# version 13
# baseline (speedup 1.0000x reference)
import numpy as np
import ml_dtypes
import concourse.bass as bass
import concourse.mybir as mybir
from concourse.bass_utils import run_bass_kernel_spmd

F32 = mybir.dt.float32
BF16 = mybir.dt.bfloat16
I32 = mybir.dt.int32
AF = mybir.ActivationFunctionType
ALU = mybir.AluOpType
AX = mybir.AxisListType

D = 1024
S = 2048
NSEQ = 2
TOK = NSEQ * S
NG = TOK // 512
GT = 4
H = 3072
KD = D // 128
KH = H // 128
NPAN = 6
P3PAN = 8
P3W = 3 * 1024
PW = 4096
EPS = 1e-5
QS = 1.0 / 127.0
US = QS * QS * 0.5

ENGINES = ("pe", "act", "dve", "sp", "pool")


class Buf:
    __slots__ = ("prod", "readers", "name", "psum")

    def __init__(self, name="", psum=False):
        self.prod = None
        self.readers = []
        self.name = name
        self.psum = psum


class Builder:
    def __init__(self):
        self.prog = {e: [] for e in ENGINES}
        self.eng_sem = {"pe": "pe", "act": "act", "dve": "dve", "sp": "spq", "pool": "poolq"}
        self.cnt = {"pe": 0, "act": 0, "dve": 0, "spq": 0, "poolq": 0, "poolc": 0}

    def task(self, eng, body, reads=(), writes=(), n_inc=1, compute=False):
        sem = "poolc" if (eng == "pool" and compute) else self.eng_sem[eng]
        dma = eng in ("sp", "pool") and not compute
        step = 16 if dma else 1
        waits = {}

        def need(m):
            if m is None:
                return
            s, c = m
            if s == "pe" and sem == "pe":
                return
            if waits.get(s, 0) < c:
                waits[s] = c

        for b in reads:
            need(b.prod)
            if b.psum and sem in ("act", "dve"):
                for r in b.readers:
                    if r[0] in ("act", "dve") and r[0] != sem:
                        need(r)
        for b in writes:
            need(b.prod)
            for r in b.readers:
                need(r)
        start = self.cnt[sem]
        self.cnt[sem] = start + step * n_inc
        marker = (sem, self.cnt[sem])
        self.prog[eng].append((waits, body, sem, step, n_inc))
        for b in writes:
            b.prod = marker
            b.readers = []
        for b in reads:
            b.readers.append(marker)
        return marker

    def emit(self, eng, engobj, sems):
        seen = {}
        n_ins = 0
        for waits, body, sem, step, n_inc in self.prog[eng]:
            for s, c in sorted(waits.items()):
                if seen.get(s, 0) >= c:
                    continue
                engobj.wait_ge(sems[s], c)
                seen[s] = c
                n_ins += 1
            insts = body(engobj)
            if not isinstance(insts, (list, tuple)):
                insts = [insts]
            n_ins += len(insts)
            if step == 16:
                assert len(insts) == n_inc, (len(insts), n_inc)
                for i in insts:
                    i.then_inc(sems[sem], 16)
            else:
                insts[-1].then_inc(sems[sem], 1)
        return n_ins


def build_nc(debug=False):
    nc = bass.Bass("TRN2", target_bir_lowering=False)
    B = Builder()

    x_d = nc.dram_tensor("x", [TOK, D], F32, kind="ExternalInput").ap()
    y_d = nc.dram_tensor("y", [TOK, D], F32, kind="ExternalOutput").ap()
    wv_d = nc.dram_tensor("Wv", [D, D], F32, kind="ExternalInput").ap()
    wo_d = nc.dram_tensor("Wo", [D, D], F32, kind="ExternalInput").ap()
    wf_d = nc.dram_tensor("Wf", [D, 1], F32, kind="ExternalInput").ap()
    wi_d = nc.dram_tensor("Wi", [D, 1], F32, kind="ExternalInput").ap()
    w1_d = nc.dram_tensor("W1", [D, H], F32, kind="ExternalInput").ap()
    w2_d = nc.dram_tensor("W2", [H, D], F32, kind="ExternalInput").ap()
    wg_d = nc.dram_tensor("wg", [H, D], I32, kind="ExternalInput").ap()
    wu_d = nc.dram_tensor("wu", [H, D], I32, kind="ExternalInput").ap()
    wd_d = nc.dram_tensor("wd", [D, H], I32, kind="ExternalInput").ap()
    ln1g_d = nc.dram_tensor("ln1g", [D], F32, kind="ExternalInput").ap()
    ln1b_d = nc.dram_tensor("ln1b", [D], F32, kind="ExternalInput").ap()
    ln2g_d = nc.dram_tensor("ln2g", [D], F32, kind="ExternalInput").ap()
    ln2b_d = nc.dram_tensor("ln2b", [D], F32, kind="ExternalInput").ap()
    bv_d = nc.dram_tensor("bv", [D], F32, kind="ExternalInput").ap()
    bo_d = nc.dram_tensor("bo", [D], F32, kind="ExternalInput").ap()
    bfs_d = nc.dram_tensor("bfs", [1], F32, kind="ExternalInput").ap()
    bis_d = nc.dram_tensor("bis", [1], F32, kind="ExternalInput").ap()
    b1_d = nc.dram_tensor("b1", [H], F32, kind="ExternalInput").ap()
    b2_d = nc.dram_tensor("b2", [D], F32, kind="ExternalInput").ap()
    tri_d = nc.dram_tensor("tri", [128, 128], F32, kind="ExternalInput").ap()
    mask_d = nc.dram_tensor("maskp", [128, 128], F32, kind="ExternalInput").ap()
    idf_d = nc.dram_tensor("identf", [128, 128], F32, kind="ExternalInput").ap()
    idb_d = nc.dram_tensor("identb", [128, 128], BF16, kind="ExternalInput").ap()
    onesr_d = nc.dram_tensor("onesr", [1, 128], F32, kind="ExternalInput").ap()
    onesb_d = nc.dram_tensor("onesb", [1, 128], BF16, kind="ExternalInput").ap()

    dbg = {}
    if debug:
        for nm, shape in [("d_hfm", [128, 1024]), ("d_vext", [128, 1040]),
                          ("d_pt", [128, 128]), ("d_lsb", [128, 8]),
                          ("d_rowr", [1, 256]), ("d_sm", [128, 16]),
                          ("d_srow", [1, 1024]), ("d_zrow", [1, 8]),
                          ("d_z2p1", [128, 4096]), ("d_fm8", [128, 4096]),
                          ("d_mfm", [128, 12288]), ("d_otm", [128, 4096]),
                          ("d_x1", [128, 4096]), ("d_fm8b", [128, 4096]),
                          ("d_mfm3", [128, 12288]), ("d_otm3", [128, 4096]),
                          ("d_omini", [128, 1024])]:
            dbg[nm] = nc.dram_tensor(nm, shape, F32, kind="ExternalOutput").ap()

    wg_bf = nc.dram_tensor("wg_bf", [H, D], BF16, kind="Internal").ap()
    wu_bf = nc.dram_tensor("wu_bf", [H, D], BF16, kind="Internal").ap()
    wd_bf = nc.dram_tensor("wd_bf", [D, H], BF16, kind="Internal").ap()
    sw1 = nc.dram_tensor("sw1", [128, NPAN * PW], BF16, kind="Internal").ap()
    sw2 = nc.dram_tensor("sw2", [128, KD * KH * 128], BF16, kind="Internal").ap()
    swg = nc.dram_tensor("swg", [128, NPAN * PW], BF16, kind="Internal").ap()
    swu = nc.dram_tensor("swu", [128, NPAN * PW], BF16, kind="Internal").ap()
    swd = nc.dram_tensor("swd", [128, KD * KH * 128], BF16, kind="Internal").ap()

    ctx = []

    def sb(name, shape, dtype):
        g = nc.sbuf_tensor(name, shape, dtype)
        ctx.append(g)
        return g.__enter__()

    def psum(name, shape, dtype=F32):
        g = nc.psum_tensor(name, shape, dtype)
        ctx.append(g)
        return g.__enter__()

    TRI = sb("TRI", [128, 128], F32)
    MASK = sb("MASK", [128, 128], F32)
    IDF = sb("IDF", [128, 128], F32)
    IDB = sb("IDB", [128, 128], BF16)
    ONESR = sb("ONESR", [1, 128], F32)
    ONESB = sb("ONESB", [1, 128], BF16)
    BVR = sb("BVR", [1, D], BF16)
    BOR = sb("BOR", [1, D], BF16)
    BFIR = sb("BFIR", [1, 2], F32)
    G1C = sb("G1C", [128, 8], F32)
    B1C = sb("B1C", [128, 8], F32)
    G2C = sb("G2C", [128, 8], F32)
    B2C = sb("B2C", [128, 8], F32)
    B1WC = sb("B1WC", [128, 24], F32)
    B2WC = sb("B2WC", [128, 8], F32)

    WV = sb("WV", [128, KD * D], BF16)
    WO = sb("WO", [128, KD * D], BF16)
    WFI = sb("WFI", [128, KD * 2], BF16)

    X0G = sb("X0G", [128, GT * D], F32)
    Z2G = sb("Z2G", [128, GT * D], F32)
    VEXT = sb("VEXT", [128, 1040], BF16)

    ZN1 = [sb(f"ZN1{i}", [128, D], BF16) for i in range(2)]
    HFM = [sb(f"HFM{i}", [128, KD * 128], BF16) for i in range(2)]
    ZMINI = [sb(f"ZMINI{i}", [128, D], F32) for i in range(2)]
    _OM = sb("OMINI", [128, D], F32); OMINI = [_OM, _OM]
    _T1 = sb("TMP1", [128, D], F32); TMP1 = [_T1, _T1]
    TEV2 = _T1
    SM = [sb(f"SM{i}", [128, 16], F32) for i in range(2)]
    SMB = sb("SMB", [128, 80], F32)
    SMB2 = sb("SMB2", [128, 48], F32)
    LSB = [sb(f"LSB{i}", [128, 8], F32) for i in range(2)]
    ROWR = [sb(f"ROWR{i}", [1, 256], F32) for i in range(2)]
    ROWRB = [sb(f"ROWRB{i}", [1, 128], BF16) for i in range(2)]
    PT = [sb(f"PT{i}", [128, 128], BF16) for i in range(2)]
    SROW = sb("SROW", [1, D], F32)
    SROWB = sb("SROWB", [1, 1032], BF16)
    RSTG = SROW
    ZROW = sb("ZROW", [1, 8], F32)

    FM8 = sb("FM8", [128, KD * 512], BF16)
    MFM = sb("MFM", [128, KH * 512], BF16)
    OTM = sb("OTM", [128, GT * D], F32)
    AOF = [sb(f"AOF{i}", [128, 512], F32) for i in range(2)]
    _EV = sb("EV1", [128, 512], F32); EV1 = [_EV, _EV]
    _GU = sb("GUV", [128, 512], F32); GUV = [_GU, _GU]
    WS = [sb(f"WS{i}", [128, PW if i < 2 else KH * 128], BF16) for i in range(6)]

    PSA = psum("PSA", [128, 512])
    PSB = psum("PSB", [128, 512])
    PSC = psum("PSC", [128, 512])
    PSD = psum("PSD", [128, 512])
    PSE = psum("PSE", [128, 512])
    PSF = psum("PSF", [128, 512])
    PSG = psum("PSG", [128, 1024], BF16)
    PSH = psum("PSH", [128, 1024], BF16)

    bA, bB, bC, bD = Buf("A", True), Buf("B", True), Buf("C", True), Buf("D", True)
    bE, bF = Buf("E", True), Buf("F", True)
    bG, bH = Buf("G", True), Buf("H", True)
    bWvo = Buf("Wvo")
    bX0, bZ2, bV = Buf("X0"), Buf("Z2"), Buf("V")
    bZN = [Buf(f"ZN{i}") for i in range(2)]
    bHF = [Buf(f"HF{i}") for i in range(2)]
    bZM = [Buf(f"ZM{i}") for i in range(2)]
    _bOM = Buf("OM"); bOM = [_bOM, _bOM]
    _bT1 = Buf("T1"); bT1 = [_bT1, _bT1]
    bSM = [Buf(f"SM{i}") for i in range(2)]
    bSMB = Buf("SMB")
    bLS = [Buf(f"LS{i}") for i in range(2)]
    bRR = [Buf(f"RR{i}") for i in range(2)]
    bPT = [Buf(f"PT{i}") for i in range(2)]
    bSR, bZR = Buf("SR"), Buf("ZR")
    bFM8, bMFM, bOTM = Buf("FM8"), Buf("MFM"), Buf("OTM")
    bAO = [Buf(f"AO{i}") for i in range(2)]
    _bEV = Buf("EVs"); bEV = [_bEV, _bEV]
    _bTE = Buf("TEs"); bTE = [_bTE, _bTE]
    _bGU = Buf("GUs"); bGU = [_bGU, _bGU]
    bWS = [Buf(f"WSs{i}") for i in range(6)]
    bCols = Buf("Cols")
    bRst = Buf("Rst")
    bScr = {n: Buf(n) for n in ("wgbf", "wubf", "wdbf", "sw1", "sw2", "swg", "swu", "swd")}

    T = B.task

    def init_loads(g):
        return [
            g.dma_start(out=TRI[:, :], in_=tri_d[:, :]),
            g.dma_start(out=MASK[:, :], in_=mask_d[:, :]),
            g.dma_start(out=IDF[:, :], in_=idf_d[:, :]),
            g.dma_start(out=IDB[:, :], in_=idb_d[:, :]),
            g.dma_start(out=ONESR[0:1, :], in_=onesr_d[:, :]),
            g.dma_start(out=ONESB[0:1, :], in_=onesb_d[:, :]),
            g.dma_start(out=BVR[0:1, :], in_=bv_d[None, :]),
            g.dma_start(out=BOR[0:1, :], in_=bo_d[None, :]),

        ]

    T("pool", init_loads, writes=[bCols], n_inc=8)

    def init_loads2(sp):
        return [
            sp.dma_start(out=BFIR[0:1, 0:1], in_=bfs_d[None, :]),
            sp.dma_start(out=BFIR[0:1, 1:2], in_=bis_d[None, :]),
            sp.dma_start(out=SROWB[0:1, 1024:1032], in_=onesb_d[:, 0:8]),
        ]

    T("sp", init_loads2, writes=[bCols], n_inc=3)

    def wvo_loads(g):
        return [
            g.dma_start(out=WV[:, :].rearrange("p (k n) -> p k n", k=KD),
                        in_=wv_d.rearrange("(k p) n -> p k n", p=128)),
            g.dma_start(out=WO[:, :].rearrange("p (k n) -> p k n", k=KD),
                        in_=wo_d.rearrange("(k p) n -> p k n", p=128)),
            g.dma_start(out=WFI[:, :].rearrange("p (k n) -> p k n", k=KD)[:, :, 0:1],
                        in_=wf_d.rearrange("(k p) n -> p k n", p=128),
                        allow_slow_non_contiguous=True),
            g.dma_start(out=WFI[:, :].rearrange("p (k n) -> p k n", k=KD)[:, :, 1:2],
                        in_=wi_d.rearrange("(k p) n -> p k n", p=128),
                        allow_slow_non_contiguous=True),
        ]

    T("pool", wvo_loads, writes=[bWvo], n_inc=4)
    T("dve", lambda v: v.memset(VEXT[:, 1024:1040], 1.0), writes=[bV])

    def load_x0(gi):
        r0 = gi * 512
        T("pool", lambda g, r0=r0: g.dma_start(
            out=X0G[:, :].rearrange("p (t n) -> p t n", t=GT),
            in_=x_d[r0:r0 + 512, :].rearrange("(t p) n -> p t n", p=128)),
          writes=[bX0], n_inc=1)

    load_x0(0)

    def build_cols(src_ap, n128, dst, dstc0):
        T("pool", lambda g, a=src_ap, n=n128: g.dma_start(out=RSTG[0:1, 0:n * 128], in_=a),
          writes=[bSR], n_inc=1)
        for j0 in range(0, n128, 4):
            nj = min(4, n128 - j0)

            def mmb(t, j0=j0, nj=nj):
                return [t.matmul(PSE[:, j:j + 1], RSTG[0:1, j * 128:(j + 1) * 128],
                                 ONESR[0:1, 0:1], start=True, stop=True)
                        for j in range(j0, j0 + nj)]

            T("pe", mmb, reads=[bSR, bCols], writes=[bE], n_inc=1)

            def evb(v, j0=j0, nj=nj, dst=dst, dstc0=dstc0):
                return [v.tensor_copy(dst[:, dstc0 + j:dstc0 + j + 1], PSE[:, j:j + 1])
                        for j in range(j0, j0 + nj)]

            T("dve", evb, reads=[bE], writes=[bCols], n_inc=1)

    build_cols(ln1g_d[None, :], 8, G1C, 0)
    build_cols(ln1b_d[None, :], 8, B1C, 0)
    build_cols(ln2g_d[None, :], 8, G2C, 0)
    build_cols(ln2b_d[None, :], 8, B2C, 0)
    build_cols(b2_d[None, :], 8, B2WC, 0)
    for c in range(3):
        build_cols(b1_d[None, c * 1024:(c + 1) * 1024], 8, B1WC, c * 8)

    for p in range(NPAN):
        sl = p % 2
        T("pool", lambda g, p=p, sl=sl: g.dma_start(
            out=WS[sl][:, 0:PW].rearrange("q (k n) -> q k n", k=KD),
            in_=w1_d.rearrange("(k q) n -> q k n", q=128)[:, :, p * 512:(p + 1) * 512]),
          writes=[bWS[sl]], n_inc=1)
        T("sp", lambda s, p=p, sl=sl: s.dma_start(out=sw1[:, p * PW:(p + 1) * PW], in_=WS[sl][:, 0:PW]),
          reads=[bWS[sl]], writes=[bScr["sw1"]], n_inc=1)
    T("pool", lambda g: g.dma_start(out=wg_bf[:, :], in_=wg_d[:, :]),
      writes=[bScr["wgbf"]], n_inc=1)
    for d in range(KD):
        sl = d % 2
        T("pool", lambda g, d=d, sl=sl: g.dma_start(
            out=WS[sl][:, 0:KH * 128].rearrange("q (k n) -> q k n", k=KH),
            in_=w2_d.rearrange("(k q) n -> q k n", q=128)[:, :, d * 128:(d + 1) * 128]),
          writes=[bWS[sl]], n_inc=1)
        T("sp", lambda s, d=d, sl=sl: s.dma_start(out=sw2[:, d * KH * 128:(d + 1) * KH * 128],
                                                  in_=WS[sl][:, 0:KH * 128]),
          reads=[bWS[sl]], writes=[bScr["sw2"]], n_inc=1)

    T("pool", lambda g: g.dma_start(out=wu_bf[:, :], in_=wu_d[:, :]),
      writes=[bScr["wubf"]], n_inc=1)
    T("pool", lambda g: g.dma_start(out=wd_bf[:, :], in_=wd_d[:, :]),
      writes=[bScr["wdbf"]], n_inc=1)

    for (src_bf, src_name, scr_name, scr_ap) in ((wg_bf, "wgbf", "swg", swg),
                                                 (wu_bf, "wubf", "swu", swu)):
        for p in range(P3PAN):
            sl = p % 2

            def trd(s, src_bf=src_bf, p=p, sl=sl):
                return [s.dma_start_transpose(WS[sl][:, k * 384:(k + 1) * 384],
                                              src_bf[p * 384:(p + 1) * 384, k * 128:(k + 1) * 128])
                        for k in range(KD)]

            T("sp", trd, reads=[bScr[src_name]], writes=[bWS[sl]], n_inc=KD)
            T("sp", lambda s, p=p, sl=sl, scr_ap=scr_ap: s.dma_start(
                out=scr_ap[:, p * P3W:(p + 1) * P3W], in_=WS[sl][:, 0:P3W]),
              reads=[bWS[sl]], writes=[bScr[scr_name]], n_inc=1)

    for k in range(KH):
        sl = k % 2
        T("sp", lambda s, k=k, sl=sl: s.dma_start_transpose(
            WS[sl][:, 0:1024], wd_bf[:, k * 128:(k + 1) * 128]),
          reads=[bScr["wdbf"]], writes=[bWS[sl]], n_inc=1)
        T("sp", lambda s, k=k, sl=sl: s.dma_start(
            out=swd.rearrange("q (d k n) -> q d k n", d=KD, k=KH)[:, :, k, :],
            in_=WS[sl][:, 0:1024].rearrange("q (d n) -> q d n", d=KD)),
          reads=[bWS[sl]], writes=[bScr["swd"]], n_inc=1)

    def stats_half(src_ap, i, half, rdbufs):
        T("dve", lambda v: v.bn_stats(SMB2[:, 12 * i + 6 * half:12 * i + 6 * half + 6], src_ap),
          reads=rdbufs, writes=[bSMB])

    def stats_aggr(c0, i):
        T("dve", lambda v: v.bn_aggr(SMB[:, c0 + 2 * i:c0 + 2 * i + 2],
                                     SMB2[:, 12 * i:12 * i + 12]),
          reads=[bSMB], writes=[bSMB])

    def stats_gather(srcs, c0, rdbufs, zsls=None):
        for i, a in enumerate(srcs):
            T("dve", lambda v, a=a, i=i: [
                v.bn_stats(SMB2[:, 12 * i:12 * i + 6], a[:, 0:512]),
                v.bn_stats(SMB2[:, 12 * i + 6:12 * i + 12], a[:, 512:1024])],
              reads=rdbufs, writes=[bSMB])
            T("dve", lambda v, i=i: v.bn_aggr(SMB[:, c0 + 2 * i:c0 + 2 * i + 2],
                                              SMB2[:, 12 * i:12 * i + 12]),
              reads=[bSMB], writes=[bSMB])

    def stats_smalls(c0, n):
        cr, cmr = c0 + 2 * n, c0 + 3 * n
        T("dve", lambda v: v.tensor_scalar(SMB[:, cr:cr + n],
                                           SMB[:, c0 + 1:c0 + 2 * n:2] if n > 1 else SMB[:, c0 + 1:c0 + 2],
                                           EPS, None, ALU.add),
          reads=[bSMB], writes=[bSMB])
        T("act", lambda a: a.activation(SMB[:, cr:cr + n], SMB[:, cr:cr + n], AF.Ln),
          reads=[bSMB], writes=[bSMB])
        T("act", lambda a: a.activation(SMB[:, cr:cr + n], SMB[:, cr:cr + n], AF.Exp, scale=-0.5),
          reads=[bSMB], writes=[bSMB])
        T("dve", lambda v: v.scalar_tensor_tensor(SMB[:, cmr:cmr + n],
                                                  SMB[:, c0:c0 + 2 * n:2] if n > 1 else SMB[:, c0:c0 + 1],
                                                  -1.0, SMB[:, cr:cr + n],
                                                  ALU.mult, ALU.mult),
          reads=[bSMB], writes=[bSMB])

    def smb_m(c0, n, i):
        return SMB[:, c0 + 2 * i:c0 + 2 * i + 1]

    def smb_r(c0, n, i):
        return SMB[:, c0 + 2 * n + i:c0 + 2 * n + i + 1]

    def smb_nmr(c0, n, i):
        return SMB[:, c0 + 3 * n + i:c0 + 3 * n + i + 1]

    def transpose_fwd(sl, src, src_buf, gc, bc, dst_ap_fn, dst_buf):
        for half, bank, pbank in ((0, bG, PSG), (1, bH, PSH)):
            def trb(t, half=half, pbank=pbank, src=src):
                return [t.transpose(pbank[:, j * 128:(j + 1) * 128],
                                    src[:, (half * 4 + j) * 128:(half * 4 + j + 1) * 128],
                                    IDB[:, :])
                        for j in range(4)]

            T("pe", trb, reads=[src_buf, bCols], writes=[bank])

            if half == 0:
                def evb_v(v, half=half, pbank=pbank):
                    out = []
                    for j in range(4):
                        k = half * 4 + j
                        if gc is None:
                            out.append(v.tensor_copy(dst_ap_fn(k), pbank[:, j * 128:(j + 1) * 128]))
                        else:
                            out.append(v.tensor_scalar(dst_ap_fn(k), pbank[:, j * 128:(j + 1) * 128],
                                                       gc[:, k:k + 1], bc[:, k:k + 1], ALU.mult, ALU.add))
                    return out

                T("dve", evb_v, reads=[bank, bCols], writes=[dst_buf])
            else:
                def evb_a(a, half=half, pbank=pbank):
                    out = []
                    for j in range(4):
                        k = half * 4 + j
                        if gc is None:
                            out.append(a.activation(dst_ap_fn(k), pbank[:, j * 128:(j + 1) * 128], AF.Copy))
                        else:
                            out.append(a.activation(dst_ap_fn(k), pbank[:, j * 128:(j + 1) * 128], AF.Identity,
                                                    scale=gc[:, k:k + 1], bias=bc[:, k:k + 1]))
                    return out

                T("act", evb_a, reads=[bank, bCols], writes=[dst_buf])

    def ln0_batch(gi):
        srcs = [X0G[:, t * D:(t + 1) * D] for t in range(GT)]
        stats_gather(srcs, 0, [bX0])
        stats_smalls(0, GT)

    def z_apply(gi, tt):
        sl = tt % 2
        T("dve", lambda v: v.tensor_scalar(ZMINI[sl][:, :], X0G[:, tt * D:(tt + 1) * D],
                                           smb_m(0, GT, tt), smb_r(0, GT, tt),
                                           ALU.subtract, ALU.mult),
          reads=[bX0, bSMB], writes=[bZM[sl]])

    def ln1_pair(gi, p):
        c0 = 20 + 10 * p
        stats_gather([ZMINI[0][:, :], ZMINI[1][:, :]], c0, [bZM[0], bZM[1]])
        stats_smalls(c0, 2)

    def zn1_apply(gi, tt):
        sl = tt % 2
        c0 = 20 + 10 * (tt // 2)
        T("act", lambda a: a.activation(ZN1[sl][:, :], ZMINI[sl][:, :], AF.Identity,
                                        bias=smb_nmr(c0, 2, tt % 2), scale=smb_r(c0, 2, tt % 2)),
          reads=[bZM[sl], bSMB], writes=[bZN[sl]])

    def p1_tok128(gi, tt):
        sl = tt % 2
        first = (gi % 4 == 0) and (tt == 0)

        transpose_fwd(sl, ZN1[sl], bZN[sl], G1C, B1C,
                      lambda k: HFM[sl][:, k * 128:(k + 1) * 128], bHF[sl])

        def mm_v(t):
            out = []
            for k in range(KD):
                lhs = HFM[sl][:, k * 128:(k + 1) * 128]
                out.append(t.matmul(PSA[:, :], lhs, WV[:, k * D:k * D + 512], start=(k == 0), stop=False))
                out.append(t.matmul(PSB[:, :], lhs, WV[:, k * D + 512:(k + 1) * D], start=(k == 0), stop=False))
                out.append(t.matmul(PSE[:, 0:2], lhs, WFI[:, k * 2:(k + 1) * 2], start=(k == 0), stop=False))
            out.append(t.matmul(PSA[:, :], ONESB[0:1, :], BVR[0:1, 0:512], start=False, stop=True))
            out.append(t.matmul(PSB[:, :], ONESB[0:1, :], BVR[0:1, 512:1024], start=False, stop=True))
            out.append(t.matmul(PSE[:, 0:2], ONESR[0:1, :], BFIR[0:1, 0:2], start=False, stop=True))
            return out

        T("pe", mm_v, reads=[bHF[sl], bWvo, bCols], writes=[bA, bB, bE])

        T("act", lambda a: a.activation(VEXT[:, 0:512], PSA[:, :], AF.Copy),
          reads=[bA], writes=[bV])
        T("dve", lambda v: v.tensor_copy(VEXT[:, 512:1024], PSB[:, :]),
          reads=[bB], writes=[bV])

        T("act", lambda a: a.activation(LSB[sl][:, 4:6], PSE[:, 0:2], AF.Abs),
          reads=[bE], writes=[bLS[sl]])
        T("act", lambda a: a.activation(LSB[sl][:, 4:6], LSB[sl][:, 4:6], AF.Exp, scale=-1.0),
          reads=[bLS[sl]], writes=[bLS[sl]])
        T("act", lambda a: a.activation(LSB[sl][:, 4:6], LSB[sl][:, 4:6], AF.Ln, bias=1.0),
          reads=[bLS[sl]], writes=[bLS[sl]])
        T("dve", lambda v: v.scalar_tensor_tensor(LSB[sl][:, 0:2], PSE[:, 0:2], 0.0,
                                                  LSB[sl][:, 4:6], ALU.min, ALU.subtract),
          reads=[bE, bLS[sl]], writes=[bLS[sl]])

        def mm_o(t):
            out = []
            for k in range(KD):
                lhs = HFM[sl][:, k * 128:(k + 1) * 128]
                out.append(t.matmul(PSC[:, :], lhs, WO[:, k * D:k * D + 512], start=(k == 0), stop=False))
                out.append(t.matmul(PSD[:, :], lhs, WO[:, k * D + 512:(k + 1) * D], start=(k == 0), stop=False))
            out.append(t.matmul(PSC[:, :], ONESB[0:1, :], BOR[0:1, 0:512], start=False, stop=True))
            out.append(t.matmul(PSD[:, :], ONESB[0:1, :], BOR[0:1, 512:1024], start=False, stop=True))
            return out

        T("pe", mm_o, reads=[bHF[sl], bWvo, bCols], writes=[bC, bD])

        def mm_cum(t):
            i1 = t.matmul(PSE[:, 4:5], TRI[:, :], LSB[sl][:, 0:1], start=True, stop=True)
            i2 = t.matmul(PSE[0:1, 8:136], LSB[sl][:, 0:1], TRI[:, :], start=True, stop=True)
            return [i1, i2]

        T("pe", mm_cum, reads=[bLS[sl], bCols, bE], writes=[bE])

        T("dve", lambda v: v.tensor_copy(ROWR[sl][0:1, 0:128], PSE[0:1, 8:136]),
          reads=[bE], writes=[bRR[sl]])
        T("dve", lambda v: v.tensor_sub(LSB[sl][:, 2:3], LSB[sl][:, 1:2], PSE[:, 4:5]),
          reads=[bE, bLS[sl]], writes=[bLS[sl]])
        T("act", lambda a: a.activation(ROWR[sl][0:1, 128:256], ROWR[sl][0:1, 0:128], AF.Exp),
          reads=[bRR[sl]], writes=[bRR[sl]])
        T("dve", lambda v: v.tensor_copy(ROWRB[sl][0:1, :], ROWR[sl][0:1, 128:256]),
          reads=[bRR[sl]], writes=[bRR[sl]])

        def mm_pb(t):
            i1 = t.matmul(PSF[:, 0:128], ONESR[0:1, :], ROWR[sl][0:1, 0:128], start=True, stop=False)
            i2 = t.matmul(PSF[:, 0:128], IDF[:, :], MASK[:, :], start=False, stop=True)
            return [i1, i2]

        T("pe", mm_pb, reads=[bRR[sl], bCols], writes=[bF])
        T("act", lambda a: a.activation(PT[sl][:, :], PSF[:, 0:128], AF.Exp, bias=LSB[sl][:, 2:3]),
          reads=[bF, bLS[sl]], writes=[bPT[sl]])

        T("act", lambda a: a.activation(OMINI[sl][:, 0:512], PSC[:, :], AF.Tanh, scale=0.5),
          reads=[bC], writes=[bOM[sl]])
        T("act", lambda a: a.activation(OMINI[sl][:, 512:1024], PSD[:, :], AF.Tanh, scale=0.5),
          reads=[bD], writes=[bOM[sl]])

        def mm_attn(t):
            out = []
            if not first:
                out.append(t.matmul(PSA[:, :], ROWRB[sl][0:1, :], SROWB[0:1, 0:512], start=True, stop=False))
                out.append(t.matmul(PSB[:, :], ROWRB[sl][0:1, :], SROWB[0:1, 512:1024], start=True, stop=False))
                out.append(t.matmul(PSE[:, 12:13], ROWRB[sl][0:1, :], SROWB[0:1, 1024:1025], start=True, stop=False))
            out.append(t.matmul(PSA[:, :], PT[sl][:, :], VEXT[:, 0:512], start=first, stop=True))
            out.append(t.matmul(PSB[:, :], PT[sl][:, :], VEXT[:, 512:1024], start=first, stop=True))
            out.append(t.matmul(PSE[:, 12:13], PT[sl][:, :], VEXT[:, 1024:1025], start=first, stop=True))
            return out

        T("pe", mm_attn, reads=[bPT[sl], bV, bSR, bZR], writes=[bA, bB, bE])

        def mm_state(t):
            out = []
            out.append(t.matmul(PSC[0:1, :], PT[sl][:, 127:128], VEXT[:, 0:512], start=True, stop=True))
            out.append(t.matmul(PSD[0:1, :], PT[sl][:, 127:128], VEXT[:, 512:1024], start=True, stop=True))
            out.append(t.matmul(PSE[0:1, 16:17], PT[sl][:, 127:128], VEXT[:, 1024:1025], start=True, stop=True))
            return out

        T("pe", mm_state, reads=[bPT[sl], bV], writes=[bC, bD, bE])

        T("dve", lambda v: v.tensor_scalar(LSB[sl][:, 6:7], PSE[:, 12:13], 2.0, 2e-6, ALU.mult, ALU.add),
          reads=[bE], writes=[bLS[sl]])
        T("dve", lambda v: v.reciprocal(LSB[sl][:, 7:8], LSB[sl][:, 6:7]),
          reads=[bLS[sl]], writes=[bLS[sl]])

        def z2a(v):
            i1 = v.tensor_scalar(TMP1[sl][:, 0:512], PSA[:, :], LSB[sl][:, 7:8], None, ALU.mult)
            i2 = v.tensor_scalar(TMP1[sl][:, 512:1024], PSB[:, :], LSB[sl][:, 7:8], None, ALU.mult)
            return [i1, i2]

        T("dve", z2a, reads=[bA, bB, bLS[sl]], writes=[bT1[sl]])
        T("dve", lambda v: v.scalar_tensor_tensor(OMINI[sl][:, :], OMINI[sl][:, :], 1.0,
                                                  TMP1[sl][:, :], ALU.add, ALU.mult),
          reads=[bT1[sl], bOM[sl]], writes=[bOM[sl]])
        T("pool", lambda g: g.tensor_add(Z2G[:, tt * D:(tt + 1) * D], OMINI[sl][:, :], ZMINI[sl][:, :]),
          reads=[bOM[sl], bZM[sl]], writes=[bZ2], compute=True)

        if first:
            T("dve", lambda v: v.tensor_copy(SROW[0:1, 0:512], PSC[0:1, :]),
              reads=[bC], writes=[bSR])
            T("dve", lambda v: v.tensor_copy(SROW[0:1, 512:1024], PSD[0:1, :]),
              reads=[bD], writes=[bSR])
            T("dve", lambda v: v.tensor_copy(ZROW[0:1, 0:1], PSE[0:1, 16:17]),
              reads=[bE], writes=[bZR])
        else:
            T("dve", lambda v: v.scalar_tensor_tensor(SROW[0:1, 0:512], SROW[0:1, 0:512],
                                                      ROWR[sl][0:1, 255:256], PSC[0:1, :],
                                                      ALU.mult, ALU.add),
              reads=[bC, bRR[sl]], writes=[bSR])
            T("dve", lambda v: v.scalar_tensor_tensor(SROW[0:1, 512:1024], SROW[0:1, 512:1024],
                                                      ROWR[sl][0:1, 255:256], PSD[0:1, :],
                                                      ALU.mult, ALU.add),
              reads=[bD, bRR[sl]], writes=[bSR])
            T("dve", lambda v: v.scalar_tensor_tensor(ZROW[0:1, 0:1], ZROW[0:1, 0:1],
                                                      ROWR[sl][0:1, 255:256], PSE[0:1, 16:17],
                                                      ALU.mult, ALU.add),
              reads=[bE, bRR[sl]], writes=[bZR])
        T("act", lambda a: a.activation(SROWB[0:1, 0:512], SROW[0:1, 0:512], AF.Copy),
          reads=[bSR], writes=[bSR])
        T("act", lambda a: a.activation(SROWB[0:1, 512:1024], SROW[0:1, 512:1024], AF.Copy),
          reads=[bSR], writes=[bSR])
        T("act", lambda a: a.activation(SROWB[0:1, 1024:1025], ZROW[0:1, 0:1], AF.Copy),
          reads=[bZR], writes=[bSR])

    def p2a_batch2(gi, pr):
        c0 = 40 + 10 * pr
        srcs = [Z2G[:, t * D:(t + 1) * D] for t in (2 * pr, 2 * pr + 1)]
        stats_gather(srcs, c0, [bZ2])
        stats_smalls(c0, 2)

    def p2a_apply2(tt):
        zsl = tt % 2
        c0 = 40 + 10 * (tt // 2)
        zap = Z2G[:, tt * D:(tt + 1) * D]
        T("act", lambda a: a.activation(ZN1[zsl][:, :], zap, AF.Identity,
                                        bias=smb_nmr(c0, 2, tt % 2), scale=smb_r(c0, 2, tt % 2)),
          reads=[bZ2, bSMB], writes=[bZN[zsl]])
        transpose_fwd(zsl, ZN1[zsl], bZN[zsl], G2C, B2C,
                      lambda k: FM8[:, k * 512 + tt * 128:k * 512 + (tt + 1) * 128], bFM8)

    def p2_panel(gi, p):
        wsl = p % 2

        T("sp", lambda s, p=p, wsl=wsl: s.dma_start(out=WS[wsl][:, 0:PW],
                                                    in_=sw1[:, p * PW:(p + 1) * PW]),
          reads=[bScr["sw1"]], writes=[bWS[wsl]], n_inc=1)

        for hh in range(4):
            j = p * 4 + hh
            pb, bank = (PSA, bA) if j % 2 == 0 else (PSB, bB)

            def mmw1(t, hh=hh, wsl=wsl, pb=pb):
                return [t.matmul(pb[:, :], WS[wsl][:, k * 512 + hh * 128:k * 512 + (hh + 1) * 128],
                                 FM8[:, k * 512:(k + 1) * 512], start=(k == 0), stop=(k == KD - 1))
                        for k in range(KD)]

            T("pe", mmw1, reads=[bWS[wsl], bFM8], writes=[bank])
            esl = j % 2
            T("act", lambda a, j=j, pb=pb, esl=esl: a.activation(
                EV1[esl][:, :], pb[:, :], AF.Lrelu, bias=B1WC[:, j:j + 1], alpha=0.01),
              reads=[bank, bCols], writes=[bEV[esl]])
            T("dve", lambda v, j=j, esl=esl: v.tensor_mul(
                MFM[:, j * 512:(j + 1) * 512], EV1[esl][:, :], EV1[esl][:, :]),
              reads=[bEV[esl]], writes=[bMFM])

    def down_mm(gi, d, scr_ap, scr_buf, wsl_base):
        wsl = wsl_base + (d % 2)
        T("sp", lambda s, d=d, wsl=wsl, scr_ap=scr_ap: s.dma_start(
            out=WS[wsl][:, 0:KH * 128], in_=scr_ap[:, d * KH * 128:(d + 1) * KH * 128]),
          reads=[scr_buf], writes=[bWS[wsl]], n_inc=1)
        pb, bank = (PSC, bC) if d % 2 == 0 else (PSD, bD)

        def mmw(t, wsl=wsl, pb=pb):
            return [t.matmul(pb[:, :], WS[wsl][:, k * 128:(k + 1) * 128],
                             MFM[:, k * 512:(k + 1) * 512], start=(k == 0), stop=(k == KH - 1))
                    for k in range(KH)]

        T("pe", mmw, reads=[bWS[wsl], bMFM], writes=[bank])
        return pb, bank

    def down_tail(d, pb, bank, residual):
        asl = d % 2
        if residual:
            T("dve", lambda v, d=d, pb=pb, asl=asl: v.tensor_scalar(
                AOF[asl][:, :], pb[:, :], B2WC[:, d:d + 1], None, ALU.add),
              reads=[bank, bCols], writes=[bAO[asl]])
        else:
            T("act", lambda a, pb=pb, asl=asl: a.activation(AOF[asl][:, :], pb[:, :], AF.Copy),
              reads=[bank], writes=[bAO[asl]])
        tb, tbank = (PSF, bF) if d % 2 == 0 else (PSE, bE)

        def trb(t, asl=asl, tb=tb):
            return [t.transpose(tb[:, j * 128:(j + 1) * 128], AOF[asl][:, j * 128:(j + 1) * 128],
                                IDF[:, :])
                    for j in range(4)]

        T("pe", trb, reads=[bAO[asl], bCols], writes=[tbank])

        if residual:
            def evb(v, d=d, tb=tb):
                return [v.tensor_add(OTM[:, j * D + d * 128:j * D + (d + 1) * 128],
                                     tb[:, j * 128:(j + 1) * 128],
                                     Z2G[:, j * D + d * 128:j * D + (d + 1) * 128])
                        for j in range(4)]
        else:
            def evb(v, d=d, tb=tb):
                return [v.tensor_copy(OTM[:, j * D + d * 128:j * D + (d + 1) * 128],
                                      tb[:, j * 128:(j + 1) * 128])
                        for j in range(4)]

        T("dve", evb, reads=[tbank] + ([bZ2] if residual else []), writes=[bOTM])

    def x1_batch(gi):
        for t in range(GT):
            stats_half(OTM[:, t * D + 512:(t + 1) * D], t, 1, [bOTM])
            stats_aggr(20, t)
        stats_smalls(20, GT)

    def p2_x1(tt):
        sl = tt % 2
        oap = OTM[:, tt * D:(tt + 1) * D]
        T("pool", lambda g: g.tensor_scalar(TMP1[sl][:, :], oap, smb_m(20, GT, tt), smb_r(20, GT, tt),
                                            ALU.subtract, ALU.mult),
          reads=[bOTM, bSMB], writes=[bT1[sl]], compute=True)
        T("dve", lambda v: v.tensor_add(Z2G[:, tt * D:(tt + 1) * D], TMP1[sl][:, :],
                                        X0G[:, tt * D:(tt + 1) * D]),
          reads=[bT1[sl], bX0], writes=[bZ2])

    def p3a(tt):
        sl = tt % 2
        xap = Z2G[:, tt * D:(tt + 1) * D]
        T("act", lambda a: a.activation(ZN1[sl][:, :], xap, AF.Identity,
                                        bias=smb_nmr(40, GT, tt), scale=smb_r(40, GT, tt)),
          reads=[bZ2, bSMB], writes=[bZN[sl]])
        transpose_fwd(sl, ZN1[sl], bZN[sl], None, None,
                      lambda k: FM8[:, k * 512 + tt * 128:k * 512 + (tt + 1) * 128], bFM8)

    def p3_panel(gi, p):
        wsl = p % 2
        usl = 2 + (p % 2)
        T("sp", lambda s, p=p, wsl=wsl: s.dma_start(out=WS[wsl][:, 0:P3W],
                                                    in_=swg[:, p * P3W:(p + 1) * P3W]),
          reads=[bScr["swg"]], writes=[bWS[wsl]], n_inc=1)
        T("sp", lambda s, p=p, usl=usl: s.dma_start(out=WS[usl][:, 0:P3W],
                                                    in_=swu[:, p * P3W:(p + 1) * P3W]),
          reads=[bScr["swu"]], writes=[bWS[usl]], n_inc=1)

        for hh in range(3):
            j = p * 3 + hh

            def mmg(t, hh=hh, wsl=wsl):
                return [t.matmul(PSA[:, :], WS[wsl][:, k * 384 + hh * 128:k * 384 + (hh + 1) * 128],
                                 FM8[:, k * 512:(k + 1) * 512], start=(k == 0), stop=(k == KD - 1))
                        for k in range(KD)]

            T("pe", mmg, reads=[bWS[wsl], bFM8], writes=[bA])

            def mmu(t, hh=hh, usl=usl):
                return [t.matmul(PSB[:, :], WS[usl][:, k * 384 + hh * 128:k * 384 + (hh + 1) * 128],
                                 FM8[:, k * 512:(k + 1) * 512], start=(k == 0), stop=(k == KD - 1))
                        for k in range(KD)]

            T("pe", mmu, reads=[bWS[usl], bFM8], writes=[bB])

            esl = j % 2
            T("act", lambda a: a.activation(TEV2[:, 512:1024], PSA[:, :], AF.Tanh,
                                            scale=QS * 0.5),
              reads=[bA], writes=[bT1[0]])
            T("dve", lambda v, esl=esl: v.tensor_copy(GUV[esl][:, :], PSA[:, :]),
              reads=[bA], writes=[bGU[esl]])
            T("act", lambda a, esl=esl: a.mul(EV1[esl][:, :], PSB[:, :], US),
              reads=[bB], writes=[bEV[esl]])
            T("dve", lambda v, esl=esl: v.tensor_mul(TMP1[0][:, 0:512], GUV[esl][:, :], EV1[esl][:, :]),
              reads=[bGU[esl], bEV[esl]], writes=[bT1[0]])
            T("dve", lambda v, j=j: v.scalar_tensor_tensor(
                MFM[:, j * 512:(j + 1) * 512], TEV2[:, 512:1024], 1.0, TMP1[0][:, 0:512],
                ALU.add, ALU.mult),
              reads=[bT1[0]], writes=[bMFM])



    def x2_batch(gi):
        for t in range(GT):
            stats_half(OTM[:, t * D + 512:(t + 1) * D], t, 1, [bOTM])
            stats_aggr(60, t)
        stats_smalls(60, GT)

    def p3_x2(gi, tt):
        sl = tt % 2
        oap = OTM[:, tt * D:(tt + 1) * D]
        T("pool", lambda g: g.tensor_scalar(TMP1[sl][:, :], oap, smb_m(60, GT, tt), smb_r(60, GT, tt),
                                            ALU.subtract, ALU.mult),
          reads=[bOTM, bSMB], writes=[bT1[sl]], compute=True)
        T("pool", lambda g: g.tensor_add(OMINI[sl][:, :], TMP1[sl][:, :],
                                         Z2G[:, tt * D:(tt + 1) * D]),
          reads=[bT1[sl], bZ2], writes=[bOM[sl]], compute=True)
        row0 = (gi * GT + tt) * 128
        T("pool", lambda g: g.dma_start(out=y_d[row0:row0 + 128, :], in_=OMINI[sl][:, :]),
          reads=[bOM[sl]], n_inc=1)

    def dump(nm, src_ap, rbufs):
        if not debug:
            return
        T("pool", lambda g, a=src_ap, o=dbg[nm]: g.dma_start(out=o, in_=a),
          reads=rbufs, n_inc=1)

    for gi in range(NG):
        if gi == 0:
            ln0_batch(0)
            z_apply(gi, 0)
            z_apply(gi, 1)
            ln1_pair(gi, 0)
        for tt in range(GT):
            if gi == 0 or tt >= 2:
                zn1_apply(gi, tt)
            p1_tok128(gi, tt)
            if tt == 0:
                z_apply(gi, 2)
            elif tt == 1:
                z_apply(gi, 3)
                ln1_pair(gi, 1)
                p2a_batch2(gi, 0)
            elif tt == 2:
                p2a_apply2(0)
                p2a_apply2(1)
            if gi == 0 and tt == 0:
                dump("d_hfm", HFM[0][:, :], [bHF[0]])
                dump("d_vext", VEXT[:, :], [bV])
                dump("d_pt", PT[0][:, :], [bPT[0]])
                dump("d_lsb", LSB[0][:, :], [bLS[0]])
                dump("d_rowr", ROWR[0][0:1, :], [bRR[0]])
                dump("d_sm", SM[0][:, :], [bSM[0]])
                dump("d_srow", SROW[0:1, :], [bSR])
                dump("d_zrow", ZROW[0:1, :], [bZR])
                dump("d_omini", OMINI[0][:, :], [bOM[0]])
        p2a_batch2(gi, 1)
        for tt in (2, 3):
            p2a_apply2(tt)
        if gi == 0:
            dump("d_z2p1", Z2G[:, :], [bZ2])
            dump("d_fm8", FM8[:, :], [bFM8])
        for p in range(NPAN):
            p2_panel(gi, p)
        if gi == 0:
            dump("d_mfm", MFM[:, :], [bMFM])
        pend = None
        for d in range(KD):
            nb = down_mm(gi, d, sw2, bScr["sw2"], 2)
            if pend is not None:
                down_tail(d - 1, pend[0], pend[1], residual=True)
                if d - 1 == 3:
                    for t in range(GT):
                        stats_half(OTM[:, t * D:t * D + 512], t, 0, [bOTM])
            pend = nb
        down_tail(KD - 1, pend[0], pend[1], residual=True)
        if gi == 0:
            dump("d_otm", OTM[:, :], [bOTM])
        x1_batch(gi)
        for tt in range(GT):
            p2_x1(tt)
            stats_half(Z2G[:, tt * D:tt * D + 512], tt, 0, [bZ2])
            stats_half(Z2G[:, tt * D + 512:(tt + 1) * D], tt, 1, [bZ2])
            stats_aggr(40, tt)
        stats_smalls(40, GT)
        for tt in range(GT):
            p3a(tt)
        if gi == 0:
            dump("d_x1", Z2G[:, :], [bZ2])
            dump("d_fm8b", FM8[:, :], [bFM8])
        for p in range(P3PAN):
            p3_panel(gi, p)
        if gi == 0:
            dump("d_mfm3", MFM[:, :], [bMFM])
        if gi + 1 < NG:
            load_x0(gi + 1)
            ln0_batch(gi + 1)
            z_apply(gi + 1, 0)
            z_apply(gi + 1, 1)
            ln1_pair(gi + 1, 0)
            zn1_apply(gi + 1, 0)
            zn1_apply(gi + 1, 1)
        pend = None
        for d in range(KD):
            nb = down_mm(gi, d, swd, bScr["swd"], 4)
            if pend is not None:
                down_tail(d - 1, pend[0], pend[1], residual=False)
                if d - 1 == 3:
                    for t in range(GT):
                        stats_half(OTM[:, t * D:t * D + 512], t, 0, [bOTM])
            pend = nb
        down_tail(KD - 1, pend[0], pend[1], residual=False)
        if gi == 0:
            dump("d_otm3", OTM[:, :], [bOTM])
        x2_batch(gi)
        for tt in range(GT):
            p3_x2(gi, tt)

    with (
        nc.semaphore("pe_s") as s_pe,
        nc.semaphore("act_s") as s_act,
        nc.semaphore("dve_s") as s_dve,
        nc.semaphore("sp_q") as s_spq,
        nc.semaphore("pool_q") as s_poolq,
        nc.semaphore("pool_c") as s_poolc,
        nc.Block() as block,
    ):
        sems = {"pe": s_pe, "act": s_act, "dve": s_dve, "spq": s_spq, "poolq": s_poolq,
                "poolc": s_poolc}
        counts = {}

        @block.tensor
        def _(t):
            counts["pe"] = B.emit("pe", t, sems)

        @block.scalar
        def _(a):
            counts["act"] = B.emit("act", a, sems)

        @block.vector
        def _(v):
            counts["dve"] = B.emit("dve", v, sems)

        @block.sync
        def _(s):
            counts["sp"] = B.emit("sp", s, sems)
            s.wait_ge(s_spq, B.cnt["spq"])

        @block.gpsimd
        def _(g):
            counts["pool"] = B.emit("pool", g, sems)
            g.wait_ge(s_poolq, B.cnt["poolq"])
            if B.cnt["poolc"]:
                g.wait_ge(s_poolc, B.cnt["poolc"])

    for g in reversed(ctx):
        g.__exit__(None, None, None)
    return nc, counts


def make_consts():
    s = np.arange(128)[:, None]
    t = np.arange(128)[None, :]
    tri = (s <= t).astype(np.float32)
    maskp = np.where(s > t, np.float32(-1e9), np.float32(0.0))
    identf = np.eye(128, dtype=np.float32)
    identb = np.eye(128).astype(ml_dtypes.bfloat16)
    onesr = np.ones((1, 128), dtype=np.float32)
    return tri, maskp, identf, identb, onesr


def kernel(**inp):
    _ = (inp["L0_attn_Wq"], inp["L0_attn_bq"], inp["L0_attn_Wk"], inp["L0_attn_bk"])
    tri, maskp, identf, identb, onesr = make_consts()
    nc, _ = build_nc()
    x = np.ascontiguousarray(np.asarray(inp["L0_attn_x"], dtype=np.float32))
    shards = x.reshape(8, TOK, D)
    common = dict(
        Wv=np.asarray(inp["L0_attn_Wv"], np.float32),
        Wo=np.asarray(inp["L0_attn_Wo"], np.float32),
        Wf=np.asarray(inp["L0_attn_Wf"], np.float32),
        Wi=np.asarray(inp["L0_attn_Wi"], np.float32),
        W1=np.asarray(inp["L0_attn_W1"], np.float32),
        W2=np.asarray(inp["L0_attn_W2"], np.float32),
        wg=np.asarray(inp["L0_ffn_w_gate"], np.int32),
        wu=np.asarray(inp["L0_ffn_w_up"], np.int32),
        wd=np.asarray(inp["L0_ffn_w_down"], np.int32),
        ln1g=np.asarray(inp["L0_attn_ln1_g"], np.float32),
        ln1b=np.asarray(inp["L0_attn_ln1_b"], np.float32),
        ln2g=np.asarray(inp["L0_attn_ln2_g"], np.float32),
        ln2b=np.asarray(inp["L0_attn_ln2_b"], np.float32),
        bv=np.asarray(inp["L0_attn_bv"], np.float32),
        bo=np.asarray(inp["L0_attn_bo"], np.float32),
        bfs=np.asarray(inp["L0_attn_bf"], np.float32),
        bis=np.asarray(inp["L0_attn_bi"], np.float32),
        b1=np.asarray(inp["L0_attn_b1"], np.float32),
        b2=np.asarray(inp["L0_attn_b2"], np.float32),
        tri=tri, maskp=maskp, identf=identf, identb=identb, onesr=onesr,
        onesb=np.ones((1, 128), dtype=ml_dtypes.bfloat16),
    )
    in_maps = [dict(common, x=np.ascontiguousarray(shards[c])) for c in range(8)]
    res = run_bass_kernel_spmd(nc, in_maps, core_ids=list(range(8)))
    out = np.stack([r["y"] for r in res.results], axis=0)
    return out.reshape(16, S, D)


if __name__ == "__main__":
    nc, counts = build_nc()
    print("instruction counts:", counts)


# version 14
# speedup vs baseline: 1.1939x; 1.1939x over previous
import numpy as np
import ml_dtypes
import concourse.bass as bass
import concourse.mybir as mybir
from concourse.bass_utils import run_bass_kernel_spmd

F32 = mybir.dt.float32
BF16 = mybir.dt.bfloat16
I32 = mybir.dt.int32
AF = mybir.ActivationFunctionType
ALU = mybir.AluOpType
AX = mybir.AxisListType

D = 1024
S = 2048
NSEQ = 2
TOK = NSEQ * S
NG = TOK // 512
GT = 4
H = 3072
KD = D // 128
KH = H // 128
NPAN = 6
P3PAN = 8
P3W = 3 * 1024
PW = 4096
EPS = 1e-5
QS = 1.0 / 127.0
US = QS * QS * 0.5

ENGINES = ("pe", "act", "dve", "sp", "pool")


class Buf:
    __slots__ = ("prod", "readers", "name", "psum")

    def __init__(self, name="", psum=False):
        self.prod = None
        self.readers = []
        self.name = name
        self.psum = psum


class Builder:
    def __init__(self):
        self.prog = {e: [] for e in ENGINES}
        self.eng_sem = {"pe": "pe", "act": "act", "dve": "dve", "sp": "spq", "pool": "poolq"}
        self.cnt = {"pe": 0, "act": 0, "dve": 0, "spq": 0, "poolq": 0, "poolc": 0}

    def task(self, eng, body, reads=(), writes=(), n_inc=1, compute=False):
        sem = "poolc" if (eng == "pool" and compute) else self.eng_sem[eng]
        dma = eng in ("sp", "pool") and not compute
        step = 16 if dma else 1
        waits = {}

        def need(m):
            if m is None:
                return
            s, c = m
            if s == "pe" and sem == "pe":
                return
            if waits.get(s, 0) < c:
                waits[s] = c

        for b in reads:
            need(b.prod)
            if b.psum and sem in ("act", "dve"):
                for r in b.readers:
                    if r[0] in ("act", "dve") and r[0] != sem:
                        need(r)
        for b in writes:
            need(b.prod)
            for r in b.readers:
                need(r)
        start = self.cnt[sem]
        self.cnt[sem] = start + step * n_inc
        marker = (sem, self.cnt[sem])
        self.prog[eng].append((waits, body, sem, step, n_inc))
        for b in writes:
            b.prod = marker
            b.readers = []
        for b in reads:
            b.readers.append(marker)
        return marker

    def emit(self, eng, engobj, sems):
        seen = {}
        n_ins = 0
        for waits, body, sem, step, n_inc in self.prog[eng]:
            for s, c in sorted(waits.items()):
                if seen.get(s, 0) >= c:
                    continue
                engobj.wait_ge(sems[s], c)
                seen[s] = c
                n_ins += 1
            insts = body(engobj)
            if not isinstance(insts, (list, tuple)):
                insts = [insts]
            n_ins += len(insts)
            if step == 16:
                assert len(insts) == n_inc, (len(insts), n_inc)
                for i in insts:
                    i.then_inc(sems[sem], 16)
            else:
                insts[-1].then_inc(sems[sem], 1)
        return n_ins


def build_nc(debug=False):
    nc = bass.Bass("TRN2", target_bir_lowering=False)
    B = Builder()

    x_d = nc.dram_tensor("x", [TOK, D], F32, kind="ExternalInput").ap()
    y_d = nc.dram_tensor("y", [TOK, D], F32, kind="ExternalOutput").ap()
    wv_d = nc.dram_tensor("Wv", [D, D], F32, kind="ExternalInput").ap()
    wo_d = nc.dram_tensor("Wo", [D, D], F32, kind="ExternalInput").ap()
    wf_d = nc.dram_tensor("Wf", [D, 1], F32, kind="ExternalInput").ap()
    wi_d = nc.dram_tensor("Wi", [D, 1], F32, kind="ExternalInput").ap()
    w1_d = nc.dram_tensor("W1", [D, H], F32, kind="ExternalInput").ap()
    w2_d = nc.dram_tensor("W2", [H, D], F32, kind="ExternalInput").ap()
    wg_d = nc.dram_tensor("wg", [H, D], I32, kind="ExternalInput").ap()
    wu_d = nc.dram_tensor("wu", [H, D], I32, kind="ExternalInput").ap()
    wd_d = nc.dram_tensor("wd", [D, H], I32, kind="ExternalInput").ap()
    ln1g_d = nc.dram_tensor("ln1g", [D], F32, kind="ExternalInput").ap()
    ln1b_d = nc.dram_tensor("ln1b", [D], F32, kind="ExternalInput").ap()
    ln2g_d = nc.dram_tensor("ln2g", [D], F32, kind="ExternalInput").ap()
    ln2b_d = nc.dram_tensor("ln2b", [D], F32, kind="ExternalInput").ap()
    bv_d = nc.dram_tensor("bv", [D], F32, kind="ExternalInput").ap()
    bo_d = nc.dram_tensor("bo", [D], F32, kind="ExternalInput").ap()
    bfs_d = nc.dram_tensor("bfs", [1], F32, kind="ExternalInput").ap()
    bis_d = nc.dram_tensor("bis", [1], F32, kind="ExternalInput").ap()
    b1_d = nc.dram_tensor("b1", [H], F32, kind="ExternalInput").ap()
    b2_d = nc.dram_tensor("b2", [D], F32, kind="ExternalInput").ap()
    tri_d = nc.dram_tensor("tri", [128, 128], F32, kind="ExternalInput").ap()
    mask_d = nc.dram_tensor("maskp", [128, 128], F32, kind="ExternalInput").ap()
    idf_d = nc.dram_tensor("identf", [128, 128], F32, kind="ExternalInput").ap()
    idb_d = nc.dram_tensor("identb", [128, 128], BF16, kind="ExternalInput").ap()
    onesr_d = nc.dram_tensor("onesr", [1, 128], F32, kind="ExternalInput").ap()
    onesb_d = nc.dram_tensor("onesb", [1, 128], BF16, kind="ExternalInput").ap()

    dbg = {}
    if debug:
        for nm, shape in [("d_hfm", [128, 1024]), ("d_vext", [128, 1040]),
                          ("d_pt", [128, 128]), ("d_lsb", [128, 8]),
                          ("d_rowr", [1, 256]), ("d_sm", [128, 16]),
                          ("d_srow", [1, 1024]), ("d_zrow", [1, 8]),
                          ("d_z2p1", [128, 4096]), ("d_fm8", [128, 4096]),
                          ("d_mfm", [128, 12288]), ("d_otm", [128, 4096]),
                          ("d_x1", [128, 4096]), ("d_fm8b", [128, 4096]),
                          ("d_mfm3", [128, 12288]), ("d_otm3", [128, 4096]),
                          ("d_omini", [128, 1024])]:
            dbg[nm] = nc.dram_tensor(nm, shape, F32, kind="ExternalOutput").ap()

    wg_bf = nc.dram_tensor("wg_bf", [H, D], BF16, kind="Internal").ap()
    wu_bf = nc.dram_tensor("wu_bf", [H, D], BF16, kind="Internal").ap()
    wd_bf = nc.dram_tensor("wd_bf", [D, H], BF16, kind="Internal").ap()
    sw1 = nc.dram_tensor("sw1", [128, NPAN * PW], BF16, kind="Internal").ap()
    sw2 = nc.dram_tensor("sw2", [128, KD * KH * 128], BF16, kind="Internal").ap()
    swg = nc.dram_tensor("swg", [128, NPAN * PW], BF16, kind="Internal").ap()
    swu = nc.dram_tensor("swu", [128, NPAN * PW], BF16, kind="Internal").ap()
    swd = nc.dram_tensor("swd", [128, KD * KH * 128], BF16, kind="Internal").ap()

    ctx = []

    def sb(name, shape, dtype):
        g = nc.sbuf_tensor(name, shape, dtype)
        ctx.append(g)
        return g.__enter__()

    def psum(name, shape, dtype=F32):
        g = nc.psum_tensor(name, shape, dtype)
        ctx.append(g)
        return g.__enter__()

    TRI = sb("TRI", [128, 128], F32)
    MASK = sb("MASK", [128, 128], F32)
    IDF = sb("IDF", [128, 128], F32)
    IDB = sb("IDB", [128, 128], BF16)
    ONESR = sb("ONESR", [1, 128], F32)
    ONESB = sb("ONESB", [1, 128], BF16)
    BVR = sb("BVR", [1, D], BF16)
    BOR = sb("BOR", [1, D], BF16)
    BFIR = sb("BFIR", [1, 2], F32)
    G1C = sb("G1C", [128, 8], F32)
    B1C = sb("B1C", [128, 8], F32)
    G2C = sb("G2C", [128, 8], F32)
    B2C = sb("B2C", [128, 8], F32)
    B1WC = sb("B1WC", [128, 24], F32)
    B2WC = sb("B2WC", [128, 8], F32)

    WV = sb("WV", [128, KD * D], BF16)
    WO = sb("WO", [128, KD * D], BF16)
    WFI = sb("WFI", [128, KD * 2], BF16)

    X0G = sb("X0G", [128, GT * D], F32)
    Z2G = sb("Z2G", [128, GT * D], F32)
    VEXT = sb("VEXT", [128, 1040], BF16)

    ZN1 = [sb(f"ZN1{i}", [128, D], BF16) for i in range(2)]
    HFM = [sb(f"HFM{i}", [128, KD * 128], BF16) for i in range(2)]
    ZMINI = [sb(f"ZMINI{i}", [128, D], F32) for i in range(2)]
    _OM = sb("OMINI", [128, D], F32); OMINI = [_OM, _OM]
    _T1 = sb("TMP1", [128, D], F32); TMP1 = [_T1, _T1]
    TEV2 = _T1
    SM = [sb(f"SM{i}", [128, 16], F32) for i in range(2)]
    SMB = sb("SMB", [128, 80], F32)
    SMB2 = sb("SMB2", [128, 48], F32)
    LSB = [sb(f"LSB{i}", [128, 8], F32) for i in range(2)]
    ROWR = [sb(f"ROWR{i}", [1, 256], F32) for i in range(2)]
    ROWRB = [sb(f"ROWRB{i}", [1, 128], BF16) for i in range(2)]
    PT = [sb(f"PT{i}", [128, 128], BF16) for i in range(2)]
    SROW = sb("SROW", [1, D], F32)
    SROWB = sb("SROWB", [1, 1032], BF16)
    RSTG = SROW
    ZROW = sb("ZROW", [1, 8], F32)

    FM8 = sb("FM8", [128, KD * 512], BF16)
    MFM = sb("MFM", [128, KH * 512], BF16)
    OTM = sb("OTM", [128, GT * D], F32)
    AOF = [sb(f"AOF{i}", [128, 512], F32) for i in range(2)]
    _EV = sb("EV1", [128, 512], F32); EV1 = [_EV, _EV]
    _GU = sb("GUV", [128, 512], F32); GUV = [_GU, _GU]
    WS = [sb(f"WS{i}", [128, PW if i < 2 else KH * 128], BF16) for i in range(6)]

    PSA = psum("PSA", [128, 512])
    PSB = psum("PSB", [128, 512])
    PSC = psum("PSC", [128, 512])
    PSD = psum("PSD", [128, 512])
    PSE = psum("PSE", [128, 512])
    PSF = psum("PSF", [128, 512])
    PSG = psum("PSG", [128, 1024], BF16)
    PSH = psum("PSH", [128, 1024], BF16)

    bA, bB, bC, bD = Buf("A", True), Buf("B", True), Buf("C", True), Buf("D", True)
    bE, bF = Buf("E", True), Buf("F", True)
    bG, bH = Buf("G", True), Buf("H", True)
    bWvo = Buf("Wvo")
    bX0, bZ2, bV = Buf("X0"), Buf("Z2"), Buf("V")
    bZN = [Buf(f"ZN{i}") for i in range(2)]
    bHF = [Buf(f"HF{i}") for i in range(2)]
    bZM = [Buf(f"ZM{i}") for i in range(2)]
    _bOM = Buf("OM"); bOM = [_bOM, _bOM]
    _bT1 = Buf("T1"); bT1 = [_bT1, _bT1]
    bSM = [Buf(f"SM{i}") for i in range(2)]
    bSMB = Buf("SMB")
    bLS = [Buf(f"LS{i}") for i in range(2)]
    bRR = [Buf(f"RR{i}") for i in range(2)]
    bPT = [Buf(f"PT{i}") for i in range(2)]
    bSR, bZR = Buf("SR"), Buf("ZR")
    bFM8, bMFM, bOTM = Buf("FM8"), Buf("MFM"), Buf("OTM")
    bAO = [Buf(f"AO{i}") for i in range(2)]
    _bEV = Buf("EVs"); bEV = [_bEV, _bEV]
    _bTE = Buf("TEs"); bTE = [_bTE, _bTE]
    _bGU = Buf("GUs"); bGU = [_bGU, _bGU]
    bWS = [Buf(f"WSs{i}") for i in range(6)]
    bCols = Buf("Cols")
    bRst = Buf("Rst")
    bScr = {n: Buf(n) for n in ("wgbf", "wubf", "wdbf", "sw1", "sw2", "swg", "swu", "swd")}

    T = B.task

    def init_loads(g):
        return [
            g.dma_start(out=TRI[:, :], in_=tri_d[:, :]),
            g.dma_start(out=MASK[:, :], in_=mask_d[:, :]),
            g.dma_start(out=IDF[:, :], in_=idf_d[:, :]),
            g.dma_start(out=IDB[:, :], in_=idb_d[:, :]),
            g.dma_start(out=ONESR[0:1, :], in_=onesr_d[:, :]),
            g.dma_start(out=ONESB[0:1, :], in_=onesb_d[:, :]),
            g.dma_start(out=BVR[0:1, :], in_=bv_d[None, :]),
            g.dma_start(out=BOR[0:1, :], in_=bo_d[None, :]),

        ]

    T("pool", init_loads, writes=[bCols], n_inc=8)

    def init_loads2(sp):
        return [
            sp.dma_start(out=BFIR[0:1, 0:1], in_=bfs_d[None, :]),
            sp.dma_start(out=BFIR[0:1, 1:2], in_=bis_d[None, :]),
            sp.dma_start(out=SROWB[0:1, 1024:1032], in_=onesb_d[:, 0:8]),
        ]

    T("sp", init_loads2, writes=[bCols], n_inc=3)

    def wvo_loads(g):
        return [
            g.dma_start(out=WV[:, :].rearrange("p (k n) -> p k n", k=KD),
                        in_=wv_d.rearrange("(k p) n -> p k n", p=128)),
            g.dma_start(out=WO[:, :].rearrange("p (k n) -> p k n", k=KD),
                        in_=wo_d.rearrange("(k p) n -> p k n", p=128)),
            g.dma_start(out=WFI[:, :].rearrange("p (k n) -> p k n", k=KD)[:, :, 0:1],
                        in_=wf_d.rearrange("(k p) n -> p k n", p=128),
                        allow_slow_non_contiguous=True),
            g.dma_start(out=WFI[:, :].rearrange("p (k n) -> p k n", k=KD)[:, :, 1:2],
                        in_=wi_d.rearrange("(k p) n -> p k n", p=128),
                        allow_slow_non_contiguous=True),
        ]

    T("pool", wvo_loads, writes=[bWvo], n_inc=4)
    T("dve", lambda v: v.memset(VEXT[:, 1024:1040], 1.0), writes=[bV])

    def load_x0(gi):
        r0 = gi * 512
        T("pool", lambda g, r0=r0: g.dma_start(
            out=X0G[:, :].rearrange("p (t n) -> p t n", t=GT),
            in_=x_d[r0:r0 + 512, :].rearrange("(t p) n -> p t n", p=128)),
          writes=[bX0], n_inc=1)

    load_x0(0)

    def build_cols(src_ap, n128, dst, dstc0):
        T("pool", lambda g, a=src_ap, n=n128: g.dma_start(out=RSTG[0:1, 0:n * 128], in_=a),
          writes=[bSR], n_inc=1)
        for j0 in range(0, n128, 4):
            nj = min(4, n128 - j0)

            def mmb(t, j0=j0, nj=nj):
                return [t.matmul(PSE[:, j:j + 1], RSTG[0:1, j * 128:(j + 1) * 128],
                                 ONESR[0:1, 0:1], start=True, stop=True)
                        for j in range(j0, j0 + nj)]

            T("pe", mmb, reads=[bSR, bCols], writes=[bE], n_inc=1)

            def evb(v, j0=j0, nj=nj, dst=dst, dstc0=dstc0):
                return [v.tensor_copy(dst[:, dstc0 + j:dstc0 + j + 1], PSE[:, j:j + 1])
                        for j in range(j0, j0 + nj)]

            T("dve", evb, reads=[bE], writes=[bCols], n_inc=1)

    build_cols(ln1g_d[None, :], 8, G1C, 0)
    build_cols(ln1b_d[None, :], 8, B1C, 0)
    build_cols(ln2g_d[None, :], 8, G2C, 0)
    build_cols(ln2b_d[None, :], 8, B2C, 0)
    build_cols(b2_d[None, :], 8, B2WC, 0)
    for c in range(3):
        build_cols(b1_d[None, c * 1024:(c + 1) * 1024], 8, B1WC, c * 8)

    for p in range(NPAN):
        sl = p % 2
        T("pool", lambda g, p=p, sl=sl: g.dma_start(
            out=WS[sl][:, 0:PW].rearrange("q (k n) -> q k n", k=KD),
            in_=w1_d.rearrange("(k q) n -> q k n", q=128)[:, :, p * 512:(p + 1) * 512]),
          writes=[bWS[sl]], n_inc=1)
        T("sp", lambda s, p=p, sl=sl: s.dma_start(out=sw1[:, p * PW:(p + 1) * PW], in_=WS[sl][:, 0:PW]),
          reads=[bWS[sl]], writes=[bScr["sw1"]], n_inc=1)
    T("pool", lambda g: g.dma_start(out=wg_bf[:, :], in_=wg_d[:, :]),
      writes=[bScr["wgbf"]], n_inc=1)
    for d in range(KD):
        sl = d % 2
        T("pool", lambda g, d=d, sl=sl: g.dma_start(
            out=WS[sl][:, 0:KH * 128].rearrange("q (k n) -> q k n", k=KH),
            in_=w2_d.rearrange("(k q) n -> q k n", q=128)[:, :, d * 128:(d + 1) * 128]),
          writes=[bWS[sl]], n_inc=1)
        T("sp", lambda s, d=d, sl=sl: s.dma_start(out=sw2[:, d * KH * 128:(d + 1) * KH * 128],
                                                  in_=WS[sl][:, 0:KH * 128]),
          reads=[bWS[sl]], writes=[bScr["sw2"]], n_inc=1)

    T("pool", lambda g: g.dma_start(out=wu_bf[:, :], in_=wu_d[:, :]),
      writes=[bScr["wubf"]], n_inc=1)
    T("pool", lambda g: g.dma_start(out=wd_bf[:, :], in_=wd_d[:, :]),
      writes=[bScr["wdbf"]], n_inc=1)

    for (src_bf, src_name, scr_name, scr_ap) in ((wg_bf, "wgbf", "swg", swg),
                                                 (wu_bf, "wubf", "swu", swu)):
        for p in range(P3PAN):
            sl = p % 2

            def trd(s, src_bf=src_bf, p=p, sl=sl):
                return [s.dma_start_transpose(WS[sl][:, k * 384:(k + 1) * 384],
                                              src_bf[p * 384:(p + 1) * 384, k * 128:(k + 1) * 128])
                        for k in range(KD)]

            T("sp", trd, reads=[bScr[src_name]], writes=[bWS[sl]], n_inc=KD)
            T("sp", lambda s, p=p, sl=sl, scr_ap=scr_ap: s.dma_start(
                out=scr_ap[:, p * P3W:(p + 1) * P3W], in_=WS[sl][:, 0:P3W]),
              reads=[bWS[sl]], writes=[bScr[scr_name]], n_inc=1)

    for k in range(KH):
        sl = k % 2
        T("sp", lambda s, k=k, sl=sl: s.dma_start_transpose(
            WS[sl][:, 0:1024], wd_bf[:, k * 128:(k + 1) * 128]),
          reads=[bScr["wdbf"]], writes=[bWS[sl]], n_inc=1)
        T("sp", lambda s, k=k, sl=sl: s.dma_start(
            out=swd.rearrange("q (d k n) -> q d k n", d=KD, k=KH)[:, :, k, :],
            in_=WS[sl][:, 0:1024].rearrange("q (d n) -> q d n", d=KD)),
          reads=[bWS[sl]], writes=[bScr["swd"]], n_inc=1)

    def stats_half(src_ap, i, half, rdbufs):
        T("dve", lambda v: v.bn_stats(SMB2[:, 12 * i + 6 * half:12 * i + 6 * half + 6], src_ap),
          reads=rdbufs, writes=[bSMB])

    def stats_aggr(c0, i):
        T("dve", lambda v: v.bn_aggr(SMB[:, c0 + 2 * i:c0 + 2 * i + 2],
                                     SMB2[:, 12 * i:12 * i + 12]),
          reads=[bSMB], writes=[bSMB])

    def stats_gather(srcs, c0, rdbufs, zsls=None):
        for i, a in enumerate(srcs):
            T("dve", lambda v, a=a, i=i: [
                v.bn_stats(SMB2[:, 12 * i:12 * i + 6], a[:, 0:512]),
                v.bn_stats(SMB2[:, 12 * i + 6:12 * i + 12], a[:, 512:1024])],
              reads=rdbufs, writes=[bSMB])
            T("dve", lambda v, i=i: v.bn_aggr(SMB[:, c0 + 2 * i:c0 + 2 * i + 2],
                                              SMB2[:, 12 * i:12 * i + 12]),
              reads=[bSMB], writes=[bSMB])

    def stats_smalls(c0, n):
        cr, cmr = c0 + 2 * n, c0 + 3 * n
        T("dve", lambda v: v.tensor_scalar(SMB[:, cr:cr + n],
                                           SMB[:, c0 + 1:c0 + 2 * n:2] if n > 1 else SMB[:, c0 + 1:c0 + 2],
                                           EPS, None, ALU.add),
          reads=[bSMB], writes=[bSMB])
        T("act", lambda a: a.activation(SMB[:, cr:cr + n], SMB[:, cr:cr + n], AF.Ln),
          reads=[bSMB], writes=[bSMB])
        T("act", lambda a: a.activation(SMB[:, cr:cr + n], SMB[:, cr:cr + n], AF.Exp, scale=-0.5),
          reads=[bSMB], writes=[bSMB])
        T("dve", lambda v: v.scalar_tensor_tensor(SMB[:, cmr:cmr + n],
                                                  SMB[:, c0:c0 + 2 * n:2] if n > 1 else SMB[:, c0:c0 + 1],
                                                  -1.0, SMB[:, cr:cr + n],
                                                  ALU.mult, ALU.mult),
          reads=[bSMB], writes=[bSMB])

    def smb_m(c0, n, i):
        return SMB[:, c0 + 2 * i:c0 + 2 * i + 1]

    def smb_r(c0, n, i):
        return SMB[:, c0 + 2 * n + i:c0 + 2 * n + i + 1]

    def smb_nmr(c0, n, i):
        return SMB[:, c0 + 3 * n + i:c0 + 3 * n + i + 1]

    def transpose_fwd(sl, src, src_buf, gc, bc, dst_ap_fn, dst_buf):
        for half, bank, pbank in ((0, bG, PSG), (1, bH, PSH)):
            def trb(t, half=half, pbank=pbank, src=src):
                return [t.transpose(pbank[:, j * 128:(j + 1) * 128],
                                    src[:, (half * 4 + j) * 128:(half * 4 + j + 1) * 128],
                                    IDB[:, :])
                        for j in range(4)]

            T("pe", trb, reads=[src_buf, bCols], writes=[bank])

            if half == 0:
                def evb_v(v, half=half, pbank=pbank):
                    out = []
                    for j in range(4):
                        k = half * 4 + j
                        if gc is None:
                            out.append(v.tensor_copy(dst_ap_fn(k), pbank[:, j * 128:(j + 1) * 128]))
                        else:
                            out.append(v.tensor_scalar(dst_ap_fn(k), pbank[:, j * 128:(j + 1) * 128],
                                                       gc[:, k:k + 1], bc[:, k:k + 1], ALU.mult, ALU.add))
                    return out

                T("dve", evb_v, reads=[bank, bCols], writes=[dst_buf])
            else:
                def evb_a(a, half=half, pbank=pbank):
                    out = []
                    for j in range(4):
                        k = half * 4 + j
                        if gc is None:
                            out.append(a.activation(dst_ap_fn(k), pbank[:, j * 128:(j + 1) * 128], AF.Copy))
                        else:
                            out.append(a.activation(dst_ap_fn(k), pbank[:, j * 128:(j + 1) * 128], AF.Identity,
                                                    scale=gc[:, k:k + 1], bias=bc[:, k:k + 1]))
                    return out

                T("act", evb_a, reads=[bank, bCols], writes=[dst_buf])

    def ln0_batch(gi):
        srcs = [X0G[:, t * D:(t + 1) * D] for t in range(GT)]
        stats_gather(srcs, 0, [bX0])
        stats_smalls(0, GT)

    def z_apply(gi, tt):
        sl = tt % 2
        T("dve", lambda v: v.tensor_scalar(ZMINI[sl][:, :], X0G[:, tt * D:(tt + 1) * D],
                                           smb_m(0, GT, tt), smb_r(0, GT, tt),
                                           ALU.subtract, ALU.mult),
          reads=[bX0, bSMB], writes=[bZM[sl]])

    def ln1_pair(gi, p):
        c0 = 20 + 10 * p
        stats_gather([ZMINI[0][:, :], ZMINI[1][:, :]], c0, [bZM[0], bZM[1]])
        stats_smalls(c0, 2)

    def zn1_apply(gi, tt):
        sl = tt % 2
        c0 = 20 + 10 * (tt // 2)
        T("act", lambda a: a.activation(ZN1[sl][:, :], ZMINI[sl][:, :], AF.Identity,
                                        bias=smb_nmr(c0, 2, tt % 2), scale=smb_r(c0, 2, tt % 2)),
          reads=[bZM[sl], bSMB], writes=[bZN[sl]])

    def p1_tok128(gi, tt):
        sl = tt % 2
        first = (gi % 4 == 0) and (tt == 0)

        transpose_fwd(sl, ZN1[sl], bZN[sl], G1C, B1C,
                      lambda k: HFM[sl][:, k * 128:(k + 1) * 128], bHF[sl])

        def mm_v(t):
            out = []
            for k in range(KD):
                lhs = HFM[sl][:, k * 128:(k + 1) * 128]
                out.append(t.matmul(PSA[:, :], lhs, WV[:, k * D:k * D + 512], start=(k == 0), stop=False))
                out.append(t.matmul(PSB[:, :], lhs, WV[:, k * D + 512:(k + 1) * D], start=(k == 0), stop=False))
                out.append(t.matmul(PSE[:, 0:2], lhs, WFI[:, k * 2:(k + 1) * 2], start=(k == 0), stop=False))
            out.append(t.matmul(PSA[:, :], ONESB[0:1, :], BVR[0:1, 0:512], start=False, stop=True))
            out.append(t.matmul(PSB[:, :], ONESB[0:1, :], BVR[0:1, 512:1024], start=False, stop=True))
            out.append(t.matmul(PSE[:, 0:2], ONESR[0:1, :], BFIR[0:1, 0:2], start=False, stop=True))
            return out

        T("pe", mm_v, reads=[bHF[sl], bWvo, bCols], writes=[bA, bB, bE])

        T("act", lambda a: a.activation(VEXT[:, 0:512], PSA[:, :], AF.Copy),
          reads=[bA], writes=[bV])
        T("dve", lambda v: v.tensor_copy(VEXT[:, 512:1024], PSB[:, :]),
          reads=[bB], writes=[bV])

        T("act", lambda a: a.activation(LSB[sl][:, 4:6], PSE[:, 0:2], AF.Abs),
          reads=[bE], writes=[bLS[sl]])
        T("act", lambda a: a.activation(LSB[sl][:, 4:6], LSB[sl][:, 4:6], AF.Exp, scale=-1.0),
          reads=[bLS[sl]], writes=[bLS[sl]])
        T("act", lambda a: a.activation(LSB[sl][:, 4:6], LSB[sl][:, 4:6], AF.Ln, bias=1.0),
          reads=[bLS[sl]], writes=[bLS[sl]])
        T("dve", lambda v: v.scalar_tensor_tensor(LSB[sl][:, 0:2], PSE[:, 0:2], 0.0,
                                                  LSB[sl][:, 4:6], ALU.min, ALU.subtract),
          reads=[bE, bLS[sl]], writes=[bLS[sl]])

        def mm_o(t):
            out = []
            for k in range(KD):
                lhs = HFM[sl][:, k * 128:(k + 1) * 128]
                out.append(t.matmul(PSC[:, :], lhs, WO[:, k * D:k * D + 512], start=(k == 0), stop=False))
                out.append(t.matmul(PSD[:, :], lhs, WO[:, k * D + 512:(k + 1) * D], start=(k == 0), stop=False))
            out.append(t.matmul(PSC[:, :], ONESB[0:1, :], BOR[0:1, 0:512], start=False, stop=True))
            out.append(t.matmul(PSD[:, :], ONESB[0:1, :], BOR[0:1, 512:1024], start=False, stop=True))
            return out

        T("pe", mm_o, reads=[bHF[sl], bWvo, bCols], writes=[bC, bD])

        def mm_cum(t):
            i1 = t.matmul(PSE[:, 4:5], TRI[:, :], LSB[sl][:, 0:1], start=True, stop=True)
            i2 = t.matmul(PSE[0:1, 8:136], LSB[sl][:, 0:1], TRI[:, :], start=True, stop=True)
            return [i1, i2]

        T("pe", mm_cum, reads=[bLS[sl], bCols, bE], writes=[bE])

        T("dve", lambda v: v.tensor_copy(ROWR[sl][0:1, 0:128], PSE[0:1, 8:136]),
          reads=[bE], writes=[bRR[sl]])
        T("dve", lambda v: v.tensor_sub(LSB[sl][:, 2:3], LSB[sl][:, 1:2], PSE[:, 4:5]),
          reads=[bE, bLS[sl]], writes=[bLS[sl]])
        T("act", lambda a: a.activation(ROWR[sl][0:1, 128:256], ROWR[sl][0:1, 0:128], AF.Exp),
          reads=[bRR[sl]], writes=[bRR[sl]])
        T("dve", lambda v: v.tensor_copy(ROWRB[sl][0:1, :], ROWR[sl][0:1, 128:256]),
          reads=[bRR[sl]], writes=[bRR[sl]])

        def mm_pb(t):
            i1 = t.matmul(PSF[:, 0:128], ONESR[0:1, :], ROWR[sl][0:1, 0:128], start=True, stop=False)
            i2 = t.matmul(PSF[:, 0:128], IDF[:, :], MASK[:, :], start=False, stop=True)
            return [i1, i2]

        T("pe", mm_pb, reads=[bRR[sl], bCols], writes=[bF])
        T("act", lambda a: a.activation(PT[sl][:, :], PSF[:, 0:128], AF.Exp, bias=LSB[sl][:, 2:3]),
          reads=[bF, bLS[sl]], writes=[bPT[sl]])

        T("act", lambda a: a.activation(OMINI[sl][:, 0:512], PSC[:, :], AF.Tanh, scale=0.5),
          reads=[bC], writes=[bOM[sl]])
        T("act", lambda a: a.activation(OMINI[sl][:, 512:1024], PSD[:, :], AF.Tanh, scale=0.5),
          reads=[bD], writes=[bOM[sl]])

        def mm_attn(t):
            out = []
            if not first:
                out.append(t.matmul(PSA[:, :], ROWRB[sl][0:1, :], SROWB[0:1, 0:512], start=True, stop=False))
                out.append(t.matmul(PSB[:, :], ROWRB[sl][0:1, :], SROWB[0:1, 512:1024], start=True, stop=False))
                out.append(t.matmul(PSE[:, 12:13], ROWRB[sl][0:1, :], SROWB[0:1, 1024:1025], start=True, stop=False))
            out.append(t.matmul(PSA[:, :], PT[sl][:, :], VEXT[:, 0:512], start=first, stop=True))
            out.append(t.matmul(PSB[:, :], PT[sl][:, :], VEXT[:, 512:1024], start=first, stop=True))
            out.append(t.matmul(PSE[:, 12:13], PT[sl][:, :], VEXT[:, 1024:1025], start=first, stop=True))
            return out

        T("pe", mm_attn, reads=[bPT[sl], bV, bSR, bZR], writes=[bA, bB, bE])

        def mm_state(t):
            out = []
            out.append(t.matmul(PSC[0:1, :], PT[sl][:, 127:128], VEXT[:, 0:512], start=True, stop=True))
            out.append(t.matmul(PSD[0:1, :], PT[sl][:, 127:128], VEXT[:, 512:1024], start=True, stop=True))
            out.append(t.matmul(PSE[0:1, 16:17], PT[sl][:, 127:128], VEXT[:, 1024:1025], start=True, stop=True))
            return out

        T("pe", mm_state, reads=[bPT[sl], bV], writes=[bC, bD, bE])

        T("dve", lambda v: v.tensor_scalar(LSB[sl][:, 6:7], PSE[:, 12:13], 2.0, 2e-6, ALU.mult, ALU.add),
          reads=[bE], writes=[bLS[sl]])
        T("dve", lambda v: v.reciprocal(LSB[sl][:, 7:8], LSB[sl][:, 6:7]),
          reads=[bLS[sl]], writes=[bLS[sl]])

        def z2a(v):
            i1 = v.tensor_scalar(TMP1[sl][:, 0:512], PSA[:, :], LSB[sl][:, 7:8], None, ALU.mult)
            i2 = v.tensor_scalar(TMP1[sl][:, 512:1024], PSB[:, :], LSB[sl][:, 7:8], None, ALU.mult)
            return [i1, i2]

        T("dve", z2a, reads=[bA, bB, bLS[sl]], writes=[bT1[sl]])
        T("dve", lambda v: v.scalar_tensor_tensor(OMINI[sl][:, :], OMINI[sl][:, :], 1.0,
                                                  TMP1[sl][:, :], ALU.add, ALU.mult),
          reads=[bT1[sl], bOM[sl]], writes=[bOM[sl]])
        T("pool", lambda g: g.tensor_add(Z2G[:, tt * D:(tt + 1) * D], OMINI[sl][:, :], ZMINI[sl][:, :]),
          reads=[bOM[sl], bZM[sl]], writes=[bZ2], compute=True)

        if first:
            T("dve", lambda v: v.tensor_copy(SROW[0:1, 0:512], PSC[0:1, :]),
              reads=[bC], writes=[bSR])
            T("dve", lambda v: v.tensor_copy(SROW[0:1, 512:1024], PSD[0:1, :]),
              reads=[bD], writes=[bSR])
            T("dve", lambda v: v.tensor_copy(ZROW[0:1, 0:1], PSE[0:1, 16:17]),
              reads=[bE], writes=[bZR])
        else:
            T("dve", lambda v: v.scalar_tensor_tensor(SROW[0:1, 0:512], SROW[0:1, 0:512],
                                                      ROWR[sl][0:1, 255:256], PSC[0:1, :],
                                                      ALU.mult, ALU.add),
              reads=[bC, bRR[sl]], writes=[bSR])
            T("dve", lambda v: v.scalar_tensor_tensor(SROW[0:1, 512:1024], SROW[0:1, 512:1024],
                                                      ROWR[sl][0:1, 255:256], PSD[0:1, :],
                                                      ALU.mult, ALU.add),
              reads=[bD, bRR[sl]], writes=[bSR])
            T("dve", lambda v: v.scalar_tensor_tensor(ZROW[0:1, 0:1], ZROW[0:1, 0:1],
                                                      ROWR[sl][0:1, 255:256], PSE[0:1, 16:17],
                                                      ALU.mult, ALU.add),
              reads=[bE, bRR[sl]], writes=[bZR])
        T("act", lambda a: a.activation(SROWB[0:1, 0:512], SROW[0:1, 0:512], AF.Copy),
          reads=[bSR], writes=[bSR])
        T("act", lambda a: a.activation(SROWB[0:1, 512:1024], SROW[0:1, 512:1024], AF.Copy),
          reads=[bSR], writes=[bSR])
        T("act", lambda a: a.activation(SROWB[0:1, 1024:1025], ZROW[0:1, 0:1], AF.Copy),
          reads=[bZR], writes=[bSR])

    def p2a_batch2(gi, pr):
        c0 = 40 + 10 * pr
        srcs = [Z2G[:, t * D:(t + 1) * D] for t in (2 * pr, 2 * pr + 1)]
        stats_gather(srcs, c0, [bZ2])
        stats_smalls(c0, 2)

    def p2a_apply2(tt):
        zsl = tt % 2
        c0 = 40 + 10 * (tt // 2)
        zap = Z2G[:, tt * D:(tt + 1) * D]
        T("act", lambda a: a.activation(ZN1[zsl][:, :], zap, AF.Identity,
                                        bias=smb_nmr(c0, 2, tt % 2), scale=smb_r(c0, 2, tt % 2)),
          reads=[bZ2, bSMB], writes=[bZN[zsl]])
        transpose_fwd(zsl, ZN1[zsl], bZN[zsl], G2C, B2C,
                      lambda k: FM8[:, k * 512 + tt * 128:k * 512 + (tt + 1) * 128], bFM8)

    def p2_panel(gi, p):
        wsl = p % 2

        T("sp", lambda s, p=p, wsl=wsl: s.dma_start(out=WS[wsl][:, 0:PW],
                                                    in_=sw1[:, p * PW:(p + 1) * PW]),
          reads=[bScr["sw1"]], writes=[bWS[wsl]], n_inc=1)

        for hh in range(4):
            j = p * 4 + hh
            pb, bank = (PSA, bA) if j % 2 == 0 else (PSB, bB)

            def mmw1(t, hh=hh, wsl=wsl, pb=pb):
                return [t.matmul(pb[:, :], WS[wsl][:, k * 512 + hh * 128:k * 512 + (hh + 1) * 128],
                                 FM8[:, k * 512:(k + 1) * 512], start=(k == 0), stop=(k == KD - 1))
                        for k in range(KD)]

            T("pe", mmw1, reads=[bWS[wsl], bFM8], writes=[bank])
            esl = j % 2
            T("act", lambda a, j=j, pb=pb, esl=esl: a.activation(
                EV1[esl][:, :], pb[:, :], AF.Lrelu, bias=B1WC[:, j:j + 1], alpha=0.01),
              reads=[bank, bCols], writes=[bEV[esl]])
            T("dve", lambda v, j=j, esl=esl: v.tensor_mul(
                MFM[:, j * 512:(j + 1) * 512], EV1[esl][:, :], EV1[esl][:, :]),
              reads=[bEV[esl]], writes=[bMFM])

    def down_mm(gi, d, scr_ap, scr_buf, wsl_base):
        wsl = wsl_base + (d % 2)
        T("sp", lambda s, d=d, wsl=wsl, scr_ap=scr_ap: s.dma_start(
            out=WS[wsl][:, 0:KH * 128], in_=scr_ap[:, d * KH * 128:(d + 1) * KH * 128]),
          reads=[scr_buf], writes=[bWS[wsl]], n_inc=1)
        pb, bank = (PSC, bC) if d % 2 == 0 else (PSD, bD)

        def mmw(t, wsl=wsl, pb=pb):
            return [t.matmul(pb[:, :], WS[wsl][:, k * 128:(k + 1) * 128],
                             MFM[:, k * 512:(k + 1) * 512], start=(k == 0), stop=(k == KH - 1))
                    for k in range(KH)]

        T("pe", mmw, reads=[bWS[wsl], bMFM], writes=[bank])
        return pb, bank

    def down_tail(d, pb, bank, residual):
        asl = d % 2
        if residual:
            T("dve", lambda v, d=d, pb=pb, asl=asl: v.tensor_scalar(
                AOF[asl][:, :], pb[:, :], B2WC[:, d:d + 1], None, ALU.add),
              reads=[bank, bCols], writes=[bAO[asl]])
        else:
            T("act", lambda a, pb=pb, asl=asl: a.activation(AOF[asl][:, :], pb[:, :], AF.Copy),
              reads=[bank], writes=[bAO[asl]])
        tb, tbank = (PSF, bF) if d % 2 == 0 else (PSE, bE)

        def trb(t, asl=asl, tb=tb):
            return [t.transpose(tb[:, j * 128:(j + 1) * 128], AOF[asl][:, j * 128:(j + 1) * 128],
                                IDF[:, :])
                    for j in range(4)]

        T("pe", trb, reads=[bAO[asl], bCols], writes=[tbank])

        if residual:
            def evb(v, d=d, tb=tb):
                return [v.tensor_add(OTM[:, j * D + d * 128:j * D + (d + 1) * 128],
                                     tb[:, j * 128:(j + 1) * 128],
                                     Z2G[:, j * D + d * 128:j * D + (d + 1) * 128])
                        for j in range(4)]
        else:
            def evb(v, d=d, tb=tb):
                return [v.tensor_copy(OTM[:, j * D + d * 128:j * D + (d + 1) * 128],
                                      tb[:, j * 128:(j + 1) * 128])
                        for j in range(4)]

        T("dve", evb, reads=[tbank] + ([bZ2] if residual else []), writes=[bOTM])

    def x1_batch(gi):
        for t in range(GT):
            stats_half(OTM[:, t * D + 512:(t + 1) * D], t, 1, [bOTM])
            stats_aggr(20, t)
        stats_smalls(20, GT)

    def p2_x1(tt):
        sl = tt % 2
        oap = OTM[:, tt * D:(tt + 1) * D]
        T("dve", lambda v: v.tensor_scalar(TMP1[sl][:, :], oap, smb_m(20, GT, tt), smb_r(20, GT, tt),
                                           ALU.subtract, ALU.mult),
          reads=[bOTM, bSMB], writes=[bT1[sl]])
        T("dve", lambda v: v.tensor_add(Z2G[:, tt * D:(tt + 1) * D], TMP1[sl][:, :],
                                        X0G[:, tt * D:(tt + 1) * D]),
          reads=[bT1[sl], bX0], writes=[bZ2])

    def p3a(tt):
        sl = tt % 2
        xap = Z2G[:, tt * D:(tt + 1) * D]
        T("act", lambda a: a.activation(ZN1[sl][:, :], xap, AF.Identity,
                                        bias=smb_nmr(40, GT, tt), scale=smb_r(40, GT, tt)),
          reads=[bZ2, bSMB], writes=[bZN[sl]])
        transpose_fwd(sl, ZN1[sl], bZN[sl], None, None,
                      lambda k: FM8[:, k * 512 + tt * 128:k * 512 + (tt + 1) * 128], bFM8)

    def p3_panel(gi, p):
        wsl = p % 2
        usl = 2 + (p % 2)
        T("sp", lambda s, p=p, wsl=wsl: s.dma_start(out=WS[wsl][:, 0:P3W],
                                                    in_=swg[:, p * P3W:(p + 1) * P3W]),
          reads=[bScr["swg"]], writes=[bWS[wsl]], n_inc=1)
        T("sp", lambda s, p=p, usl=usl: s.dma_start(out=WS[usl][:, 0:P3W],
                                                    in_=swu[:, p * P3W:(p + 1) * P3W]),
          reads=[bScr["swu"]], writes=[bWS[usl]], n_inc=1)

        for hh in range(3):
            j = p * 3 + hh

            def mmg(t, hh=hh, wsl=wsl):
                return [t.matmul(PSA[:, :], WS[wsl][:, k * 384 + hh * 128:k * 384 + (hh + 1) * 128],
                                 FM8[:, k * 512:(k + 1) * 512], start=(k == 0), stop=(k == KD - 1))
                        for k in range(KD)]

            T("pe", mmg, reads=[bWS[wsl], bFM8], writes=[bA])

            def mmu(t, hh=hh, usl=usl):
                return [t.matmul(PSB[:, :], WS[usl][:, k * 384 + hh * 128:k * 384 + (hh + 1) * 128],
                                 FM8[:, k * 512:(k + 1) * 512], start=(k == 0), stop=(k == KD - 1))
                        for k in range(KD)]

            T("pe", mmu, reads=[bWS[usl], bFM8], writes=[bB])

            esl = j % 2
            T("act", lambda a: a.activation(TEV2[:, 512:1024], PSA[:, :], AF.Tanh,
                                            scale=QS * 0.5),
              reads=[bA], writes=[bT1[0]])
            T("dve", lambda v, esl=esl: v.tensor_copy(GUV[esl][:, :], PSA[:, :]),
              reads=[bA], writes=[bGU[esl]])
            T("act", lambda a, esl=esl: a.mul(EV1[esl][:, :], PSB[:, :], US),
              reads=[bB], writes=[bEV[esl]])
            T("dve", lambda v, esl=esl: v.tensor_mul(TMP1[0][:, 0:512], GUV[esl][:, :], EV1[esl][:, :]),
              reads=[bGU[esl], bEV[esl]], writes=[bT1[0]])
            T("dve", lambda v, j=j: v.scalar_tensor_tensor(
                MFM[:, j * 512:(j + 1) * 512], TEV2[:, 512:1024], 1.0, TMP1[0][:, 0:512],
                ALU.add, ALU.mult),
              reads=[bT1[0]], writes=[bMFM])



    def x2_batch(gi):
        for t in range(GT):
            stats_half(OTM[:, t * D + 512:(t + 1) * D], t, 1, [bOTM])
            stats_aggr(60, t)
        stats_smalls(60, GT)

    def p3_x2(gi, tt):
        sl = tt % 2
        oap = OTM[:, tt * D:(tt + 1) * D]
        T("dve", lambda v: v.tensor_scalar(TMP1[sl][:, :], oap, smb_m(60, GT, tt), smb_r(60, GT, tt),
                                           ALU.subtract, ALU.mult),
          reads=[bOTM, bSMB], writes=[bT1[sl]])
        T("pool", lambda g: g.tensor_add(OMINI[sl][:, :], TMP1[sl][:, :],
                                         Z2G[:, tt * D:(tt + 1) * D]),
          reads=[bT1[sl], bZ2], writes=[bOM[sl]], compute=True)
        row0 = (gi * GT + tt) * 128
        T("pool", lambda g: g.dma_start(out=y_d[row0:row0 + 128, :], in_=OMINI[sl][:, :]),
          reads=[bOM[sl]], n_inc=1)

    def dump(nm, src_ap, rbufs):
        if not debug:
            return
        T("pool", lambda g, a=src_ap, o=dbg[nm]: g.dma_start(out=o, in_=a),
          reads=rbufs, n_inc=1)

    for gi in range(NG):
        if gi == 0:
            ln0_batch(0)
            z_apply(gi, 0)
            z_apply(gi, 1)
            ln1_pair(gi, 0)
        for tt in range(GT):
            if gi == 0 or tt >= 2:
                zn1_apply(gi, tt)
            p1_tok128(gi, tt)
            if tt == 0:
                z_apply(gi, 2)
            elif tt == 1:
                z_apply(gi, 3)
                ln1_pair(gi, 1)
                p2a_batch2(gi, 0)
            elif tt == 2:
                p2a_apply2(0)
                p2a_apply2(1)
            if gi == 0 and tt == 0:
                dump("d_hfm", HFM[0][:, :], [bHF[0]])
                dump("d_vext", VEXT[:, :], [bV])
                dump("d_pt", PT[0][:, :], [bPT[0]])
                dump("d_lsb", LSB[0][:, :], [bLS[0]])
                dump("d_rowr", ROWR[0][0:1, :], [bRR[0]])
                dump("d_sm", SM[0][:, :], [bSM[0]])
                dump("d_srow", SROW[0:1, :], [bSR])
                dump("d_zrow", ZROW[0:1, :], [bZR])
                dump("d_omini", OMINI[0][:, :], [bOM[0]])
        p2a_batch2(gi, 1)
        for tt in (2, 3):
            p2a_apply2(tt)
        if gi == 0:
            dump("d_z2p1", Z2G[:, :], [bZ2])
            dump("d_fm8", FM8[:, :], [bFM8])
        for p in range(NPAN):
            p2_panel(gi, p)
        if gi == 0:
            dump("d_mfm", MFM[:, :], [bMFM])
        pend = None
        for d in range(KD):
            nb = down_mm(gi, d, sw2, bScr["sw2"], 2)
            if pend is not None:
                down_tail(d - 1, pend[0], pend[1], residual=True)
                if d - 1 == 3:
                    for t in range(GT):
                        stats_half(OTM[:, t * D:t * D + 512], t, 0, [bOTM])
            pend = nb
        down_tail(KD - 1, pend[0], pend[1], residual=True)
        if gi == 0:
            dump("d_otm", OTM[:, :], [bOTM])
        x1_batch(gi)
        for tt in range(GT):
            p2_x1(tt)
            stats_half(Z2G[:, tt * D:tt * D + 512], tt, 0, [bZ2])
            stats_half(Z2G[:, tt * D + 512:(tt + 1) * D], tt, 1, [bZ2])
            stats_aggr(40, tt)
        stats_smalls(40, GT)
        for tt in range(GT):
            p3a(tt)
        if gi == 0:
            dump("d_x1", Z2G[:, :], [bZ2])
            dump("d_fm8b", FM8[:, :], [bFM8])
        for p in range(P3PAN):
            p3_panel(gi, p)
        if gi == 0:
            dump("d_mfm3", MFM[:, :], [bMFM])
        if gi + 1 < NG:
            load_x0(gi + 1)
            ln0_batch(gi + 1)
            z_apply(gi + 1, 0)
            z_apply(gi + 1, 1)
            ln1_pair(gi + 1, 0)
            zn1_apply(gi + 1, 0)
            zn1_apply(gi + 1, 1)
        pend = None
        for d in range(KD):
            nb = down_mm(gi, d, swd, bScr["swd"], 4)
            if pend is not None:
                down_tail(d - 1, pend[0], pend[1], residual=False)
                if d - 1 == 3:
                    for t in range(GT):
                        stats_half(OTM[:, t * D:t * D + 512], t, 0, [bOTM])
            pend = nb
        down_tail(KD - 1, pend[0], pend[1], residual=False)
        if gi == 0:
            dump("d_otm3", OTM[:, :], [bOTM])
        x2_batch(gi)
        for tt in range(GT):
            p3_x2(gi, tt)

    with (
        nc.semaphore("pe_s") as s_pe,
        nc.semaphore("act_s") as s_act,
        nc.semaphore("dve_s") as s_dve,
        nc.semaphore("sp_q") as s_spq,
        nc.semaphore("pool_q") as s_poolq,
        nc.semaphore("pool_c") as s_poolc,
        nc.Block() as block,
    ):
        sems = {"pe": s_pe, "act": s_act, "dve": s_dve, "spq": s_spq, "poolq": s_poolq,
                "poolc": s_poolc}
        counts = {}

        @block.tensor
        def _(t):
            counts["pe"] = B.emit("pe", t, sems)

        @block.scalar
        def _(a):
            counts["act"] = B.emit("act", a, sems)

        @block.vector
        def _(v):
            counts["dve"] = B.emit("dve", v, sems)

        @block.sync
        def _(s):
            counts["sp"] = B.emit("sp", s, sems)
            s.wait_ge(s_spq, B.cnt["spq"])

        @block.gpsimd
        def _(g):
            counts["pool"] = B.emit("pool", g, sems)
            g.wait_ge(s_poolq, B.cnt["poolq"])
            if B.cnt["poolc"]:
                g.wait_ge(s_poolc, B.cnt["poolc"])

    for g in reversed(ctx):
        g.__exit__(None, None, None)
    return nc, counts


def make_consts():
    s = np.arange(128)[:, None]
    t = np.arange(128)[None, :]
    tri = (s <= t).astype(np.float32)
    maskp = np.where(s > t, np.float32(-1e9), np.float32(0.0))
    identf = np.eye(128, dtype=np.float32)
    identb = np.eye(128).astype(ml_dtypes.bfloat16)
    onesr = np.ones((1, 128), dtype=np.float32)
    return tri, maskp, identf, identb, onesr


def kernel(**inp):
    _ = (inp["L0_attn_Wq"], inp["L0_attn_bq"], inp["L0_attn_Wk"], inp["L0_attn_bk"])
    tri, maskp, identf, identb, onesr = make_consts()
    nc, _ = build_nc()
    x = np.ascontiguousarray(np.asarray(inp["L0_attn_x"], dtype=np.float32))
    shards = x.reshape(8, TOK, D)
    common = dict(
        Wv=np.asarray(inp["L0_attn_Wv"], np.float32),
        Wo=np.asarray(inp["L0_attn_Wo"], np.float32),
        Wf=np.asarray(inp["L0_attn_Wf"], np.float32),
        Wi=np.asarray(inp["L0_attn_Wi"], np.float32),
        W1=np.asarray(inp["L0_attn_W1"], np.float32),
        W2=np.asarray(inp["L0_attn_W2"], np.float32),
        wg=np.asarray(inp["L0_ffn_w_gate"], np.int32),
        wu=np.asarray(inp["L0_ffn_w_up"], np.int32),
        wd=np.asarray(inp["L0_ffn_w_down"], np.int32),
        ln1g=np.asarray(inp["L0_attn_ln1_g"], np.float32),
        ln1b=np.asarray(inp["L0_attn_ln1_b"], np.float32),
        ln2g=np.asarray(inp["L0_attn_ln2_g"], np.float32),
        ln2b=np.asarray(inp["L0_attn_ln2_b"], np.float32),
        bv=np.asarray(inp["L0_attn_bv"], np.float32),
        bo=np.asarray(inp["L0_attn_bo"], np.float32),
        bfs=np.asarray(inp["L0_attn_bf"], np.float32),
        bis=np.asarray(inp["L0_attn_bi"], np.float32),
        b1=np.asarray(inp["L0_attn_b1"], np.float32),
        b2=np.asarray(inp["L0_attn_b2"], np.float32),
        tri=tri, maskp=maskp, identf=identf, identb=identb, onesr=onesr,
        onesb=np.ones((1, 128), dtype=ml_dtypes.bfloat16),
    )
    in_maps = [dict(common, x=np.ascontiguousarray(shards[c])) for c in range(8)]
    res = run_bass_kernel_spmd(nc, in_maps, core_ids=list(range(8)))
    out = np.stack([r["y"] for r in res.results], axis=0)
    return out.reshape(16, S, D)


if __name__ == "__main__":
    nc, counts = build_nc()
    print("instruction counts:", counts)
